# version 06
# baseline (speedup 1.0000x reference)
import numpy as np
from contextlib import ExitStack

import concourse.bass as bass
import concourse.mybir as mybir
from concourse.bass_utils import run_bass_kernel_spmd
from concourse.bass import ds

F32 = mybir.dt.float32
BF16 = mybir.dt.bfloat16
AF = mybir.ActivationFunctionType
ALU = mybir.AluOpType
AX = mybir.AxisListType.X

B_, T, D = 4, 4096, 1024
HID = 4096
TOK = 2048
TC = 512
NCHUNK = TOK // TC
NTI = TC // 128
H = 8
DM = 64
E = H * DM
NTT = T // 128
EPS = 1e-5

DEBUG_OUT = []


def host_prepare(inputs):
    asc = np.ascontiguousarray
    x = np.asarray(inputs["L0_attn_x"], dtype=np.float32)
    Wk = np.asarray(inputs["L0_attn_Wk"], np.float32)
    Wv = np.asarray(inputs["L0_attn_Wv"], np.float32)
    Wq = np.asarray(inputs["L0_attn_Wq"], np.float32)
    Wb = np.asarray(inputs["L0_attn_Wbeta"], np.float32)
    Wo = np.asarray(inputs["L0_attn_Wout"], np.float32)

    def tile_w(wT, kdim, odim, ocols):
        nk, nob = kdim // 128, odim // ocols
        return asc(wT.reshape(nk, 128, nob, ocols).transpose(2, 1, 0, 3))

    def pmajor(w):
        k, n = w.shape
        return asc(w.reshape(k // 128, 128, n).transpose(1, 0, 2))

    w1T = np.asarray(inputs["L0_ffn_w1_w"], np.float32).T
    w2T = np.asarray(inputs["L0_ffn_w2_w"], np.float32).T
    h1T = np.asarray(inputs["L0_ffn_h1_w"], np.float32).T
    h2T = np.asarray(inputs["L0_ffn_h2_w"], np.float32).T
    w3T = np.asarray(inputs["L0_ffn_w3_w"], np.float32).T

    shared = {
        "w1t": tile_w(w1T, D, HID, 512),
        "w2t": tile_w(w2T, D, HID, 512),
        "h1t": tile_w(h1T, HID, HID, 128),
        "h2t": tile_w(h2T, HID, HID, 128),
        "w3t": tile_w(w3T, HID, D, 128),
        "s1pc": asc(np.asarray(inputs["L0_ffn_w1_s"], np.float32).reshape(32, 128).T),
        "s2pc": asc(np.asarray(inputs["L0_ffn_w2_s"], np.float32).reshape(32, 128).T),
        "h1spc": asc(np.asarray(inputs["L0_ffn_h1_s"], np.float32).reshape(32, 128).T),
        "h2spc": asc(np.asarray(inputs["L0_ffn_h2_s"], np.float32).reshape(32, 128).T),
        "s3pc": asc(np.asarray(inputs["L0_ffn_w3_s"], np.float32).reshape(8, 128).T),
        "wout": pmajor(Wo),
    }
    in_maps = []
    for c in range(8):
        b, j = c // 2, c % 2
        m = dict(shared)
        m["x_own"] = asc(x[b, j * TOK:(j + 1) * TOK, :])
        m["wk"] = pmajor(asc(Wk[:, j * E:(j + 1) * E]))
        m["wv"] = pmajor(asc(Wv[:, j * E:(j + 1) * E]))
        m["wq"] = pmajor(asc(Wq[:, j * E:(j + 1) * E]))
        m["wbeta"] = pmajor(asc(Wb[:, j * H:(j + 1) * H]))
        in_maps.append(m)
    return in_maps


def build_nc():
    nc = bass.Bass("TRN2", target_bir_lowering=False, debug=False)
    ap = lambda t: t.ap()

    x_own = ap(nc.dram_tensor("x_own", [TOK, D], F32, kind="ExternalInput"))
    wk_d = ap(nc.dram_tensor("wk", [128, 8, E], F32, kind="ExternalInput"))
    wv_d = ap(nc.dram_tensor("wv", [128, 8, E], F32, kind="ExternalInput"))
    wq_d = ap(nc.dram_tensor("wq", [128, 8, E], F32, kind="ExternalInput"))
    wbeta_d = ap(nc.dram_tensor("wbeta", [128, 8, H], F32, kind="ExternalInput"))
    wout_d = ap(nc.dram_tensor("wout", [128, 8, D], F32, kind="ExternalInput"))
    w1t = ap(nc.dram_tensor("w1t", [8, 128, 8, 512], F32, kind="ExternalInput"))
    w2t = ap(nc.dram_tensor("w2t", [8, 128, 8, 512], F32, kind="ExternalInput"))
    h1t = ap(nc.dram_tensor("h1t", [32, 128, 32, 128], F32, kind="ExternalInput"))
    h2t = ap(nc.dram_tensor("h2t", [32, 128, 32, 128], F32, kind="ExternalInput"))
    w3t = ap(nc.dram_tensor("w3t", [8, 128, 32, 128], F32, kind="ExternalInput"))
    s1pc = ap(nc.dram_tensor("s1pc", [128, 32], F32, kind="ExternalInput"))
    s2pc = ap(nc.dram_tensor("s2pc", [128, 32], F32, kind="ExternalInput"))
    h1spc = ap(nc.dram_tensor("h1spc", [128, 32], F32, kind="ExternalInput"))
    h2spc = ap(nc.dram_tensor("h2spc", [128, 32], F32, kind="ExternalInput"))
    s3pc = ap(nc.dram_tensor("s3pc", [128, 8], F32, kind="ExternalInput"))
    y_out = ap(nc.dram_tensor("y_out", [TOK, D], F32, kind="ExternalOutput"))

    xmid_hbm = ap(nc.dram_tensor("xmid_hbm", [TOK, D], F32, kind="Internal"))
    x2_hbm = ap(nc.dram_tensor("x2_hbm", [TOK, D], F32, kind="Internal"))
    w1c = ap(nc.dram_tensor("w1c", [8, 128, 8, 512], BF16, kind="Internal"))
    w2c = ap(nc.dram_tensor("w2c", [8, 128, 8, 512], BF16, kind="Internal"))
    h1c = ap(nc.dram_tensor("h1c", [32, 128, 32, 128], BF16, kind="Internal"))
    h2c = ap(nc.dram_tensor("h2c", [32, 128, 32, 128], BF16, kind="Internal"))
    w3c = ap(nc.dram_tensor("w3c", [8, 128, 32, 128], BF16, kind="Internal"))
    cc1_in_t = [nc.dram_tensor("cc1_in%d" % i, [8, 128, TOK // 2], BF16,
                               kind="Internal") for i in range(2)]
    cc1_out_t = [nc.dram_tensor("cc1_out%d" % i, [2, 8, 128, TOK // 2], BF16,
                                kind="Internal") for i in range(2)]
    cc2_in_t = nc.dram_tensor("cc2_in", [4, 128, T], BF16, kind="Internal")
    cc2_out_t = nc.dram_tensor("cc2_out", [2, 4, 128, T], BF16, kind="Internal")

    dbg = {}
    for name in DEBUG_OUT:
        shp = {"xmid": [TOK, D], "x2": [TOK, D], "y2": [8, 128, T],
               "k": [4, 128, T], "q": [4, 128, T], "v": [32, 128, E],
               "cum": [128, 256], "beta": [128, 256],
               "attnout": [4, 128, T], "attnfull": [8, 128, TOK]}[name]
        dbg[name] = ap(nc.dram_tensor("dbg_" + name, shp, F32,
                                      kind="ExternalOutput"))

    i128f = np.eye(128, dtype=np.float32)
    consts = {
        "I128F": i128f, "I128B": i128f,
        "TRILB": np.tril(np.ones((128, 128), np.float32)),
        "TRIUF": np.triu(np.ones((128, 128), np.float32)),
        "ONESF": np.ones((128, 128), np.float32),
        "ONES1": np.ones((1, 128), np.float32),
        "SEL8": np.stack([np.repeat(np.eye(8, dtype=np.float32)[:, h:h+1], 128, 1)
                          for h in range(8)], 0).transpose(1, 0, 2).reshape(8, 1024),
        "IREP8": np.tile(i128f, (1, 8)),
        "IND64": np.stack([np.r_[np.ones(64), np.zeros(64)],
                           np.r_[np.zeros(64), np.ones(64)]], 1).astype(np.float32),
        "INDT": np.stack([np.r_[np.ones(64), np.zeros(64)],
                          np.r_[np.zeros(64), np.ones(64)]], 0).astype(np.float32),
        "CVALS": np.tile(np.array([[EPS, 1.0 + 1e-8, 1.0 / D, 0.0,
                                    1e-12, 0.5, -1.0, 1.0,
                                    -0.5, np.log(0.125), 1e-24, 0.0]],
                                   np.float32), (128, 1)),
    }
    cd = {k: ap(nc.inline_tensor(v.astype(np.float32), name="c_" + k))
          for k, v in consts.items()}

    es0 = ExitStack()
    sb0 = lambda n, s, d: es0.enter_context(nc.sbuf_tensor(n, s, d))
    P = [es0.enter_context(nc.psum_tensor("P%d" % i, [128, 512], F32))
         for i in range(6)]
    PB = [es0.enter_context(nc.psum_tensor("PB%d" % i, [128, 1024], BF16))
          for i in range(2)]

    cI128F = sb0("cI128F", [128, 128], F32)
    cI128B = sb0("cI128B", [128, 128], BF16)
    cTRILB = sb0("cTRILB", [128, 128], BF16)
    cTRIUF = sb0("cTRIUF", [128, 128], F32)
    cONESF = sb0("cONESF", [128, 128], F32)
    cONES1 = sb0("cONES1", [1, 128], F32)
    cSEL8 = sb0("cSEL8", [8, 1024], F32)
    cSEL8B = sb0("cSEL8B", [8, 1024], BF16)
    cIREP8 = sb0("cIREP8", [128, 1024], BF16)
    cIND64 = sb0("cIND64", [128, 2], F32)
    cINDT = sb0("cINDT", [2, 128], F32)
    cv = sb0("cv", [128, 12], F32)
    (CV_EPS, CV_1P, CV_INVD, CV_Z, CV_1E12, CV_HALF, CV_NEG1, CV_ONE,
     CV_NHALF, CV_LN8, CV_T24, CV_Z2) = range(12)
    s1_sb = sb0("s1_sb", [128, 32], F32)
    s2_sb = sb0("s2_sb", [128, 32], F32)
    h1s_sb = sb0("h1s_sb", [128, 32], F32)
    h2s_sb = sb0("h2s_sb", [128, 32], F32)
    s3h_sb = sb0("s3h_sb", [128, 8], F32)
    stats = sb0("stats", [128, 64], F32)
    beta_tm = sb0("beta_tm", [128, 256], F32)
    lr_tm = sb0("lr_tm", [128, 256], F32)
    cum_tm = sb0("cum_tm", [128, 256], F32)
    ncum_tm = sb0("ncum_tm", [128, 256], F32)

    sem_names = ["sMM", "sVE", "sSC", "sCC", "sCCB", "sCC2", "sCL", "sC",
                 "sX", "sXB", "sST",
                 "sCW0", "sCW1", "sCW2", "sCW3",
                 "sY2", "sPW", "sAT", "sXM0", "sXM1", "sX2A", "sX2B",
                 "sW0", "sW1", "sW2", "sW3"]
    sems = {n: es0.enter_context(nc.semaphore(n)) for n in sem_names}
    sMM, sVE, sSC, sCC = (sems[k] for k in ["sMM", "sVE", "sSC", "sCC"])
    sCL = sems["sCL"]
    sCCp = [sems["sCC"], sems["sCCB"]]
    sCC2 = sems["sCC2"]
    sC, sST = sems["sC"], sems["sST"]
    sXp = [sems["sX"], sems["sXB"]]
    sCW = [sems["sCW%d" % i] for i in range(4)]
    sY2, sPW, sAT = (sems[k] for k in ["sY2", "sPW", "sAT"])
    sXM = [sems["sXM0"], sems["sXM1"]]
    sX2 = [sems["sX2A"], sems["sX2B"]]
    sW = [sems["sW%d" % i] for i in range(4)]
    cnt = {n: 0 for n in sem_names}

    def inc(instr, sem, k=1):
        cnt[sem.name] += k
        instr.then_inc(sem, k)
        return cnt[sem.name]

    def val(sem):
        return cnt[sem.name]

    block = es0.enter_context(nc.Block())

    def on_t(fn):
        @block.tensor
        def _(t):
            fn(t)

    def on_v(fn):
        @block.vector
        def _(v):
            fn(v)

    def on_s(fn):
        @block.scalar
        def _(s):
            fn(s)

    def on_g(fn):
        @block.gpsimd
        def _(g):
            fn(g)

    def on_y(fn):
        @block.sync
        def _(y):
            fn(y)

    def ph0y(y):
        for name, tile in [("I128F", cI128F), ("TRIUF", cTRIUF),
                           ("ONESF", cONESF), ("IND64", cIND64),
                           ("INDT", cINDT), ("CVALS", cv), ("ONES1", cONES1),
                           ("SEL8", cSEL8)]:
            inc(y.dma_start(out=tile[:, :], in_=cd[name][:, :]), sC, 16)
        for dst, src in [(s1_sb, s1pc), (s2_sb, s2pc), (h1s_sb, h1spc),
                         (h2s_sb, h2spc), (s3h_sb, s3pc)]:
            inc(y.dma_start(out=dst[:, :], in_=src[:, :]), sC, 16)
    on_y(ph0y)

    def ph0g(g):
        for name, tile in [("I128B", cI128B), ("TRILB", cTRILB),
                           ("IREP8", cIREP8), ("SEL8", cSEL8B)]:
            inc(g.dma_start(out=tile[:, :], in_=cd[name][:, :]), sPW, 16)
    on_g(ph0g)
    ph0_ld, ph0_wt = val(sC), val(sPW)

    def ph0s(s):
        s.wait_ge(sC, ph0_ld)
        i = s.activation(out=s3h_sb[:, :], in_=s3h_sb[:, :], func=AF.Copy,
                         scale=cv[:, CV_HALF:CV_HALF + 1])
        inc(i, sSC, 1)
    on_s(ph0s)
    s3h_ready = val(sSC)

    def ph0t(t):
        t.wait_ge(sC, ph0_ld)
        t.wait_ge(sPW, ph0_wt)
    on_t(ph0t)

    bufs = {}
    ln_state = {}

    def emit_ln_stats(src_tm, ready_ve, ready_ld):
        ytile, sq_ln = bufs["ytile"], bufs["sq_ln"]

        def vst1(v):
            if ready_ve:
                v.wait_ge(sVE, ready_ve)
            if ready_ld:
                v.wait_ge(ready_ld[0], ready_ld[1])
            ins = None
            for i in range(NTI):
                xt = src_tm[:, i * 1024:(i + 1) * 1024]
                st = lambda c: stats[:, i * 8 + c:i * 8 + c + 1]
                v.reduce_sum(out=st(0), in_=xt, axis=AX)
                v.drain()
                ins = v.tensor_scalar(out=st(2), in0=st(0), scalar1=-1.0 / D,
                                      scalar2=None, op0=ALU.mult)
                v.drain()
            inc(ins, sVE, 1)
        on_v(vst1)
        v1 = val(sVE)

        def sst1(s):
            if ready_ve:
                s.wait_ge(sVE, ready_ve)
            if ready_ld:
                s.wait_ge(ready_ld[0], ready_ld[1])
            ins = None
            for i in range(NTI):
                s.drain()
                ins = s.activation(out=sq_ln[:, :],
                                   in_=src_tm[:, i * 1024:(i + 1) * 1024],
                                   func=AF.Square,
                                   accum_out=stats[:, i * 8 + 1:i * 8 + 2])
            inc(ins, sSC, 1)
        on_s(sst1)
        s1m = val(sSC)

        def vst2(v):
            v.wait_ge(sSC, s1m)
            v.wait_ge(sVE, v1)
            ins = None
            for i in range(NTI):
                st = lambda c: stats[:, i * 8 + c:i * 8 + c + 1]
                v.drain()
                v.tensor_tensor(out=st(3), in0=st(2), in1=st(2), op=ALU.mult)
                v.drain()
                ins = v.scalar_tensor_tensor(out=st(4), in0=st(1),
                                             scalar=1.0 / D, in1=st(3),
                                             op0=ALU.mult, op1=ALU.subtract)
            v.drain()
            inc(ins, sVE, 1)
        on_v(vst2)
        v2 = val(sVE)

        def sst2(s):
            s.wait_ge(sVE, v2)
            ins = None
            for i in range(NTI):
                st = lambda c: stats[:, i * 8 + c:i * 8 + c + 1]
                s.drain()
                ins = s.activation(out=st(5), in_=st(4), func=AF.Sqrt,
                                   bias=cv[:, CV_EPS:CV_EPS + 1])
            inc(ins, sSC, 1)
        on_s(sst2)
        s2m = val(sSC)

        def vst3(v):
            v.wait_ge(sSC, s2m)
            ins = None
            for i in range(NTI):
                st = lambda c: stats[:, i * 8 + c:i * 8 + c + 1]
                v.drain()
                v.reciprocal(out=st(6), in_=st(5))
                v.drain()
                ins = v.tensor_tensor(out=st(7), in0=st(2), in1=st(6),
                                      op=ALU.mult)
            v.drain()
            inc(ins, sVE, 1)
        on_v(vst3)

        sc0 = val(sVE)
        pmb = ln_state.get("prev_mm_base")

        def vst4(v):
            for i in range(NTI):
                st = lambda c: stats[:, i * 8 + c:i * 8 + c + 1]
                if pmb is not None:
                    v.wait_ge(sMM, pmb + i + 1)
                v.drain()
                ins = v.tensor_scalar(out=ytile[i][:, :],
                                      in0=src_tm[:, i * 1024:(i + 1) * 1024],
                                      scalar1=st(6), scalar2=st(7),
                                      op0=ALU.mult, op1=ALU.add)
                inc(ins, sVE, 1)
        on_v(vst4)
        assert val(sVE) == sc0 + NTI
        return {"sc0": sc0, "ytile": ytile}

    def emit_ln_finish(lst, dst_fm, dst_free_mm, dst_free_cc):
        ytile = lst["ytile"]
        sc0 = lst["sc0"]
        mm_base = val(sMM)
        ln_state["prev_mm_base"] = mm_base
        ve_base = val(sVE)

        def tst(t):
            t.wait_ge(sVE, ve_base)
            for i in range(NTI):
                t.wait_ge(sVE, sc0 + i + 1)
                if i >= 1:
                    t.wait_ge(sVE, ve_base + 8 * i)
                ins = None
                for dc in range(8):
                    pb = PB[dc // 4]
                    ins = t.transpose(pb[:, (dc % 4) * 128:(dc % 4) * 128 + 128],
                                      ytile[i][:, dc * 128:(dc + 1) * 128],
                                      cI128B[:, :])
                inc(ins, sMM, 1)
        on_t(tst)
        assert val(sMM) == mm_base + NTI

        def vtr(v):
            if dst_free_mm:
                v.wait_ge(sMM, dst_free_mm)
            if dst_free_cc:
                v.wait_ge(dst_free_cc[0], dst_free_cc[1])
            for i in range(NTI):
                v.wait_ge(sMM, mm_base + i + 1)
                for dc in range(8):
                    pb = PB[dc // 4]
                    ins = v.tensor_copy(
                        out=dst_fm[:, dc * TC + i * 128: dc * TC + (i + 1) * 128],
                        in_=pb[:, (dc % 4) * 128:(dc % 4) * 128 + 128])
                    inc(ins, sVE, 1)
        on_v(vtr)
        return val(sVE)

    def emit_ln(src_tm, ready_ve, ready_ld, dst_fm, dst_free_mm, dst_free_cc):
        lst = emit_ln_stats(src_tm, ready_ve, ready_ld)
        return emit_ln_finish(lst, dst_fm, dst_free_mm, dst_free_cc)

    ffn_state = {"prev_mm_end": 0, "cache_wt": 0}
    cwcnt = [0, 0, 0, 0]

    def emit_ffn(y_fm, y_ready_ve, res_tm, res_ready_ve, res_ready_ld,
                 out_tm, out_free_ve, out_free_ld, out_hbm, ck, build_cache,
                 dbg_store, mid_cb=None):
        sil, ffo, h_fm, h2_fm, wbuf = (bufs["sil"], bufs["ffo"],
                                       bufs["h_fm"], bufs["h2_fm"],
                                       bufs["wbuf"])

        t12_base = val(sMM)
        mm_w1 = lambda oc: t12_base + 2 * oc + 1
        mm_w2 = lambda oc: t12_base + 2 * oc + 2
        v12_base = val(sVE)
        ve_h = lambda oc: v12_base + oc + 1
        s12_base = val(sSC)
        sc_sil = lambda oc: s12_base + oc + 1
        prev_end = ffn_state["prev_mm_end"]

        w12m = []
        cw_jobs = []

        def g12(g):
            if not build_cache and ffn_state["cache_wt"]:
                for b in range(4):
                    g.wait_ge(sCW[b], ffn_state["cache_wt"][b])
            for ob in range(8):
                b0, b1 = 2 * (ob % 2), 2 * (ob % 2) + 1
                if ob < 2:
                    if prev_end:
                        g.wait_ge(sMM, prev_end)
                    if build_cache:
                        for b in (b0, b1):
                            if cwcnt[b]:
                                g.wait_ge(sCW[b], cwcnt[b])
                else:
                    g.wait_ge(sMM, mm_w2(4 * (ob - 2) + 3))
                    if build_cache:
                        g.wait_ge(sCW[b0], cwcnt[b0])
                        g.wait_ge(sCW[b1], cwcnt[b1])
                src1 = (w1t if build_cache else w1c)[ob]
                src2 = (w2t if build_cache else w2c)[ob]
                m1 = inc(g.dma_start(out=wbuf[b0][:, :],
                                     in_=src1.rearrange("p k c -> p (k c)")),
                         sW[b0], 16)
                m2 = inc(g.dma_start(out=wbuf[b1][:, :],
                                     in_=src2.rearrange("p k c -> p (k c)")),
                         sW[b1], 16)
                w12m.append((m1, m2))
                if build_cache:
                    for b, m, wc in ((b0, m1, w1c[ob]), (b1, m2, w2c[ob])):
                        cwcnt[b] += 16
                        cw_jobs.append((sW[b], m, wbuf[b][:, :],
                                        wc.rearrange("p k c -> p (k c)"),
                                        sCW[b]))
        on_g(g12)
        if build_cache:
            def ycw12(y, jobs=list(cw_jobs)):
                for sw, m, srcap, dstap, scw in jobs:
                    y.wait_ge(sw, m)
                    inc(y.dma_start(out=dstap, in_=srcap), scw, 16)
            on_y(ycw12)
        if mid_cb is not None:
            mid_cb_res = [None]

        def t12(t):
            t.wait_ge(sVE, y_ready_ve)
            for oc in range(32):
                ob, sub = oc // 4, oc % 4
                t.wait_ge(sW[2 * (ob % 2)], w12m[ob][0])
                t.wait_ge(sW[2 * (ob % 2) + 1], w12m[ob][1])
                if oc >= 2:
                    t.wait_ge(sVE, ve_h(oc - 2))
                pa, pb = P[2 * (oc % 2)], P[2 * (oc % 2) + 1]
                ins = None
                for k in range(8):
                    ins = t.matmul(pa[:, :],
                                   wbuf[2 * (ob % 2)][:, k * 512 + sub * 128:
                                                      k * 512 + sub * 128 + 128],
                                   y_fm[:, k * TC:(k + 1) * TC],
                                   start=(k == 0), stop=(k == 7))
                inc(ins, sMM, 1)
                for k in range(8):
                    ins = t.matmul(pb[:, :],
                                   wbuf[2 * (ob % 2) + 1][:, k * 512 + sub * 128:
                                                          k * 512 + sub * 128 + 128],
                                   y_fm[:, k * TC:(k + 1) * TC],
                                   start=(k == 0), stop=(k == 7))
                inc(ins, sMM, 1)
        on_t(t12)
        assert val(sMM) == t12_base + 64

        def s12(s):
            for oc in range(32):
                s.wait_ge(sMM, mm_w1(oc))
                if oc >= 2:
                    s.wait_ge(sVE, ve_h(oc - 2))
                s.drain()
                ins = s.activation(out=sil[oc % 2][:, :],
                                   in_=P[2 * (oc % 2)][:, :],
                                   func=AF.Silu, scale=s1_sb[:, oc:oc + 1])
                inc(ins, sSC, 1)
        on_s(s12)
        assert val(sSC) == s12_base + 32

        def v12(v):
            for oc in range(32):
                v.wait_ge(sMM, mm_w2(oc))
                v.wait_ge(sSC, sc_sil(oc))
                ins = v.scalar_tensor_tensor(
                    out=h_fm[:, oc * TC:(oc + 1) * TC],
                    in0=P[2 * (oc % 2) + 1][:, :],
                    scalar=s2_sb[:, oc:oc + 1], in1=sil[oc % 2][:, :],
                    op0=ALU.mult, op1=ALU.mult)
                inc(ins, sVE, 1)
        on_v(v12)
        assert val(sVE) == v12_base + 32
        if mid_cb is not None:
            mid_cb_res[0] = mid_cb()

        def hstage(wt_f, wt_c, src, dst, sc_sb, src_ready_kind, src_ready):
            th_base = val(sMM)
            mmg = lambda oc: th_base + oc + 1
            sh_base = val(sSC)
            scg = lambda oc: sh_base + oc + 1
            wm = []
            cwh_jobs = []

            def gh(g):
                for ob in range(32):
                    b = ob % 4
                    if ob < 4:
                        g.wait_ge(sMM, th_base)
                    else:
                        g.wait_ge(sMM, mmg(ob - 4))
                    if build_cache and cwcnt[b]:
                        g.wait_ge(sCW[b], cwcnt[b])
                    srcw = (wt_f if build_cache else wt_c)[ob]
                    m = inc(g.dma_start(out=wbuf[b][:, :],
                                        in_=srcw.rearrange("p k c -> p (k c)")),
                            sW[b], 16)
                    wm.append(m)
                    if build_cache:
                        cwcnt[b] += 16
                        cwh_jobs.append((sW[b], m, wbuf[b][:, :],
                                         wt_c[ob].rearrange("p k c -> p (k c)"),
                                         sCW[b]))

            def th(t):
                if src_ready_kind == "ve":
                    t.wait_ge(sVE, src_ready)
                else:
                    t.wait_ge(sSC, src_ready)
                for oc in range(32):
                    t.wait_ge(sW[oc % 4], wm[oc])
                    if oc >= 2:
                        t.wait_ge(sSC, scg(oc - 2))
                    pa = P[2 * (oc % 2)]
                    ins = None
                    for k in range(32):
                        ins = t.matmul(pa[:, :],
                                       wbuf[oc % 4][:, k * 128:(k + 1) * 128],
                                       src[:, k * TC:(k + 1) * TC],
                                       start=(k == 0), stop=(k == 31))
                    inc(ins, sMM, 1)

            def sh(s):
                for oc in range(32):
                    s.wait_ge(sMM, mmg(oc))
                    s.drain()
                    ins = s.activation(out=dst[:, oc * TC:(oc + 1) * TC],
                                       in_=P[2 * (oc % 2)][:, :], func=AF.Gelu,
                                       scale=sc_sb[:, oc:oc + 1])
                    inc(ins, sSC, 1)

            on_g(gh)
            if build_cache:
                def ycwh(y, jobs=list(cwh_jobs)):
                    for sw, m, srcap, dstap, scw in jobs:
                        y.wait_ge(sw, m)
                        inc(y.dma_start(out=dstap, in_=srcap), scw, 16)
                on_y(ycwh)
            on_t(th)
            on_s(sh)
            assert val(sMM) == th_base + 32
            assert val(sSC) == sh_base + 32
            return scg(31)

        sc_h1 = hstage(h1t, h1c, h_fm, h2_fm, h1s_sb, "ve", ve_h(31))
        sc_h2 = hstage(h2t, h2c, h2_fm, h_fm, h2s_sb, "sc", sc_h1)

        t3_base = val(sMM)
        mm3 = lambda oc: t3_base + (1 if oc == 0 else 2 * oc)
        tr3 = lambda oc: t3_base + (2 * oc + 3 if oc <= 6 else 16)
        s3_base = val(sSC)
        sc3 = lambda oc: s3_base + oc + 1
        v3_base = val(sVE)
        vres = lambda oc: v3_base + oc + 1
        w3m = []
        cw3_jobs = []

        def g3(g):
            for ob in range(8):
                b = ob % 4
                if ob < 4:
                    g.wait_ge(sMM, t3_base)
                else:
                    g.wait_ge(sMM, mm3(ob - 4))
                if build_cache and cwcnt[b]:
                    g.wait_ge(sCW[b], cwcnt[b])
                srcw = (w3t if build_cache else w3c)[ob]
                m = inc(g.dma_start(out=wbuf[b][:, :],
                                    in_=srcw.rearrange("p k c -> p (k c)")),
                        sW[b], 16)
                w3m.append(m)
                if build_cache:
                    cwcnt[b] += 16
                    cw3_jobs.append((sW[b], m, wbuf[b][:, :],
                                     w3c[ob].rearrange("p k c -> p (k c)"),
                                     sCW[b]))
        on_g(g3)
        if build_cache:
            def ycw3(y, jobs=list(cw3_jobs)):
                for sw, m, srcap, dstap, scw in jobs:
                    y.wait_ge(sw, m)
                    inc(y.dma_start(out=dstap, in_=srcap), scw, 16)
            on_y(ycw3)

        def emit_tr(t, oc):
            t.wait_ge(sSC, sc3(oc))
            if oc >= 2:
                t.wait_ge(sVE, vres(oc - 2))
            pb = P[4 + oc % 2]
            ins = None
            for ti in range(NTI):
                ins = t.transpose(pb[:, ti * 128:(ti + 1) * 128],
                                  ffo[oc % 2][:, ti * 128:(ti + 1) * 128],
                                  cI128F[:, :])
            inc(ins, sMM, 1)

        def t3(t):
            t.wait_ge(sSC, sc_h2)
            for oc in range(8):
                t.wait_ge(sW[oc % 4], w3m[oc])
                if oc >= 2:
                    t.wait_ge(sSC, sc3(oc - 2))
                pa = P[2 * (oc % 2)]
                ins = None
                for k in range(32):
                    ins = t.matmul(pa[:, :],
                                   wbuf[oc % 4][:, k * 128:(k + 1) * 128],
                                   h_fm[:, k * TC:(k + 1) * TC],
                                   start=(k == 0), stop=(k == 31))
                inc(ins, sMM, 1)
                if oc >= 1:
                    emit_tr(t, oc - 1)
            emit_tr(t, 7)
        on_t(t3)
        assert val(sMM) == t3_base + 16

        def s3(s):
            s.wait_ge(sSC, s3h_ready)
            for oc in range(8):
                s.wait_ge(sMM, mm3(oc))
                if oc >= 2:
                    s.wait_ge(sMM, tr3(oc - 2))
                s.drain()
                ins = s.activation(out=ffo[oc % 2][:, :],
                                   in_=P[2 * (oc % 2)][:, :], func=AF.Copy,
                                   scale=s3h_sb[:, oc:oc + 1])
                inc(ins, sSC, 1)
        on_s(s3)

        def v3(v):
            if res_ready_ve:
                v.wait_ge(sVE, res_ready_ve)
            if res_ready_ld:
                v.wait_ge(res_ready_ld[0], res_ready_ld[1])
            if out_free_ve:
                v.wait_ge(sVE, out_free_ve)
            if out_free_ld:
                v.wait_ge(out_free_ld[0], out_free_ld[1])
            for oc in range(8):
                v.wait_ge(sMM, tr3(oc))
                pb = P[4 + oc % 2]
                ins = None
                for ti in range(NTI):
                    ins = v.tensor_tensor(
                        out=out_tm[:, ti * 1024 + oc * 128: ti * 1024 + oc * 128 + 128],
                        in0=res_tm[:, ti * 1024 + oc * 128: ti * 1024 + oc * 128 + 128],
                        in1=pb[:, ti * 128:(ti + 1) * 128], op=ALU.add)
                inc(ins, sVE, 1)
        on_v(v3)
        assert val(sVE) == v3_base + 8
        out_ve = val(sVE)

        def yst(y):
            y.wait_ge(sVE, out_ve)
            for i in range(NTI):
                ins = y.dma_start(
                    out=out_hbm[ck * TC + i * 128: ck * TC + (i + 1) * 128, :],
                    in_=out_tm[:, i * 1024:(i + 1) * 1024])
                inc(ins, sST, 16)
            if dbg_store is not None:
                for i in range(NTI):
                    inc(y.dma_start(
                        out=dbg_store[ck * TC + i * 128: ck * TC + (i + 1) * 128, :],
                        in_=out_tm[:, i * 1024:(i + 1) * 1024]), sST, 16)
        on_y(yst)
        ffn_state["prev_mm_end"] = t3_base + 16
        ffn_state["t12_end"] = t12_base + 64
        if build_cache:
            ffn_state["cache_wt"] = list(cwcnt)
        ffn_state["mid_res"] = mid_cb_res[0] if mid_cb is not None else None
        return val(sST), out_ve

    esA = ExitStack()
    sbA = lambda n, s, d: esA.enter_context(nc.sbuf_tensor(n, s, d))
    bufs = {
        "lnsrc": [sbA("lnsrc%d" % i, [128, NTI * 1024], F32) for i in range(2)],
        "xout": sbA("xout", [128, NTI * 1024], F32),
        "yf": [sbA("yf%d" % i, [128, 8 * TC], BF16) for i in range(2)],
        "ytile": [sbA("ytileA%d" % i, [128, 1024], BF16) for i in range(4)],
        "sil": [sbA("silA%d" % i, [128, 512], F32) for i in range(2)],
        "ffo": [sbA("ffoA%d" % i, [128, 512], F32) for i in range(2)],
        "h_fm": sbA("h_fmA", [128, 32 * TC], BF16),
        "h2_fm": sbA("h2_fmA", [128, 32 * TC], BF16),
        "wbuf": [sbA("wbufA%d" % i, [128, 4096], BF16) for i in range(4)],
        "sq_ln": sbA("sq_lnA", [128, 1024], F32),
    }

    hist = {}

    def yx_ck(ck):
        def yx(y, ck=ck):
            if ck >= 2:
                y.wait_ge(sVE, hist[(ck - 2, "out_ve")])
            for i in range(NTI):
                inc(y.dma_start(
                    out=bufs["lnsrc"][ck % 2][:, i * 1024:(i + 1) * 1024],
                    in_=x_own[ck * TC + i * 128: ck * TC + (i + 1) * 128, :]),
                    sXp[ck % 2], 16)
        on_y(yx)
        return (sXp[ck % 2], val(sXp[ck % 2]))

    xld0 = yx_ck(0)
    hist[(0, "xld")] = xld0
    hist[(0, "lst")] = emit_ln_stats(bufs["lnsrc"][0], 0, xld0)
    hist[(0, "ln1_ve")] = emit_ln_finish(hist[(0, "lst")], bufs["yf"][0], 0, 0)
    for ck in range(NCHUNK):
        par = ck % 2
        lnsrc = bufs["lnsrc"][par]
        yf = bufs["yf"][par]

        def midcb(ck=ck):
            if ck + 1 < NCHUNK:
                xld = yx_ck(ck + 1)
                hist[(ck + 1, "xld")] = xld
                hist[(ck + 1, "lst")] = emit_ln_stats(
                    bufs["lnsrc"][(ck + 1) % 2], 0, xld)
            return None

        st_mark, out_ve = emit_ffn(
            yf, hist[(ck, "ln1_ve")], lnsrc, 0, hist.get((ck, "xld"), 0),
            bufs["xout"],
            hist.get((ck - 1, "ln2_ve"), 0),
            (sST, hist[(ck - 1, "store")]) if ck >= 1 else 0,
            xmid_hbm, ck, ck == 0, dbg.get("xmid"), mid_cb=midcb)
        hist[(ck, "t12_end")] = ffn_state["t12_end"]
        hist[(ck, "store")] = st_mark
        hist[(ck, "out_ve")] = out_ve
        if ck + 1 < NCHUNK:
            hist[(ck + 1, "ln1_ve")] = emit_ln_finish(
                hist[(ck + 1, "lst")], bufs["yf"][(ck + 1) % 2],
                hist.get((ck - 1, "t12_end"), 0),
                hist.get((ck - 1, "bounce"), 0))
        ln2_ve = emit_ln(bufs["xout"], out_ve, 0, yf,
                         ffn_state["t12_end"], hist.get((ck - 2, "bounce"), 0))
        hist[(ck, "ln2_ve")] = ln2_ve

        def gb(g, ck=ck, ln2_ve=ln2_ve, yf=yf, par=par):
            g.wait_ge(sVE, ln2_ve)
            half_t = cc1_in_t[ck // 2]
            lo = (ck % 2) * TC
            for dc in range(8):
                inc(g.dma_start(
                    out=half_t.ap()[dc, :, lo:lo + TC],
                    in_=yf[:, dc * TC:(dc + 1) * TC]), sCCp[par], 16)
        on_g(gb)
        hist[(ck, "bounce")] = (sCCp[par], val(sCCp[par]))
        if ck == 1:
            b01 = (val(sCCp[0]), val(sCCp[1]))

            def gag1a(g, b01=b01):
                g.wait_ge(sCCp[0], b01[0])
                g.wait_ge(sCCp[1], b01[1])
                inc(g.collective_compute(
                    "AllGather", ALU.bypass,
                    replica_groups=[[0, 1], [2, 3], [4, 5], [6, 7]],
                    ins=[cc1_in_t[0].ap().opt()],
                    outs=[cc1_out_t[0].ap().opt()]), sCL, 1)
            on_g(gag1a)
    bounce1 = (val(sCCp[0]), val(sCCp[1]))
    esA.close()

    esB1 = ExitStack()
    sbB1 = lambda n, s, d: esB1.enter_context(nc.sbuf_tensor(n, s, d))
    k_fm = sbB1("k_fm", [128, 4 * T], BF16)
    q_fm = sbB1("q_fm", [128, 4 * T], BF16)
    v_tm = sbB1("v_tm", [128, 32 * E], BF16)
    cum_hd = sbB1("cum_hd", [8, T], F32)
    beta_hd = sbB1("beta_hd", [8, T], BF16)

    esB2 = ExitStack()
    sbB2 = lambda n, s, d: esB2.enter_context(nc.sbuf_tensor(n, s, d))
    y2buf = sbB2("y2buf", [128, 8 * 1024], BF16)
    cum_hd2 = sbB2("cum_hd2", [8, T], F32)
    wk_sb = sbB2("wk_sb", [128, 8 * E], BF16)
    wv_sb = sbB2("wv_sb", [128, 8 * E], BF16)
    wq_sb = sbB2("wq_sb", [128, 8 * E], BF16)
    wbeta_sb = sbB2("wbeta_sb", [128, 8 * H], BF16)
    kq_scr = [sbB2("kq_scr%d" % p, [128, 512], F32) for p in range(2)]
    sqp_scr = [sbB2("sqp_scr%d" % p, [128, 512], F32) for p in range(2)]
    nrm_scr0 = sbB2("nrm_scr", [2, 512], F32)
    nrm_scr = [nrm_scr0, nrm_scr0]
    nrm2_scr = [sbB2("nrm2_scr%d" % p, [2, 512], F32) for p in range(2)]

    ph1_stores = val(sST)

    def ph2(g):
        g.wait_ge(sCCp[0], bounce1[0])
        g.wait_ge(sCCp[1], bounce1[1])
        g.wait_ge(sST, ph1_stores)
        i = g.collective_compute(
            "AllGather", ALU.bypass,
            replica_groups=[[0, 1], [2, 3], [4, 5], [6, 7]],
            ins=[cc1_in_t[1].ap().opt()], outs=[cc1_out_t[1].ap().opt()])
        inc(i, sCL, 1)
        g.wait_ge(sCL, val(sCL))
        for dst, src in [(wk_sb, wk_d), (wv_sb, wv_d), (wq_sb, wq_d),
                         (wbeta_sb, wbeta_d)]:
            inc(g.dma_start(out=dst[:, :],
                            in_=src.rearrange("p c e -> p (c e)")), sPW, 16)
    on_g(ph2)
    ag1_done = val(sCL)
    pw_ld = val(sPW)

    if "y2" in dbg:
        def dy2(g):
            g.wait_ge(sCL, ag1_done)
            for dc in range(8):
                for half in range(2):
                    for hb in range(2):
                        inc(g.dma_start(
                            out=dbg["y2"][dc, :, half * TOK + hb * 1024:
                                          half * TOK + (hb + 1) * 1024],
                            in_=cc1_out_t[hb].ap()[half, dc]), sAT, 16)
        on_g(dy2)

    def load_y2_quarter(g, qi):
        src_t = cc1_out_t[qi % 2]
        g.wait_ge(sCL, ag1_done - 1 + (qi % 2))
        half = qi // 2
        for dc in range(8):
            inc(g.dma_start(out=y2buf[:, dc * 1024:(dc + 1) * 1024],
                            in_=src_t.ap()[half, dc]),
                sY2, 16)
        return val(sY2)

    kqstate = {"buffree": 0, "idx": 0, "marks": {}}

    def emit_kq(dst_fm, w_src, is_q):
        M = kqstate["marks"]
        for qi in range(4):
            qm = [0]

            def gq(g, qi=qi, qm=qm):
                if kqstate["buffree"]:
                    g.wait_ge(sMM, kqstate["buffree"])
                qm[0] = load_y2_quarter(g, qi)
            on_g(gq)
            for ec in range(4):
                for tsl in range(2):
                    i = kqstate["idx"]
                    kqstate["idx"] += 1
                    p = i % 2
                    Pp, Pn = P[0 + 2 * p], P[1 + 2 * p]

                    def tp(t, ec=ec, tsl=tsl, qm=qm, i=i, p=p, Pp=Pp,
                           last=(ec == 3 and tsl == 1)):
                        t.wait_ge(sY2, qm[0])
                        t.wait_ge(sPW, pw_ld)
                        if (i - 2, "vf") in M:
                            t.wait_ge(sVE, M[(i - 2, "vf")])
                        ins = None
                        for d in range(8):
                            ins = t.matmul(
                                Pp[:, :],
                                w_src[:, d * E + ec * 128: d * E + ec * 128 + 128],
                                y2buf[:, d * 1024 + tsl * 512: d * 1024 + (tsl + 1) * 512],
                                start=(d == 0), stop=(d == 7))
                        M[(i, "mm")] = inc(ins, sMM, 1)
                        if last:
                            kqstate["buffree"] = val(sMM)
                    on_t(tp)

                    def sp(s, i=i, p=p, Pp=Pp):
                        s.wait_ge(sMM, M[(i, "mm")])
                        if (i - 2, "ind") in M:
                            s.wait_ge(sMM, M[(i - 2, "ind")])
                        s.drain()
                        M[(i, "sq")] = inc(
                            s.activation(out=sqp_scr[p][:, :], in_=Pp[:, :],
                                         func=AF.Square), sSC, 1)
                    on_s(sp)

                    def ti(t, i=i, p=p, Pn=Pn):
                        t.wait_ge(sSC, M[(i, "sq")])
                        if (i - 2, "bc") in M:
                            t.wait_ge(sSC, M[(i - 2, "ev")])
                        M[(i, "ind")] = inc(
                            t.matmul(Pn[0:2, :], cIND64[:, :],
                                     sqp_scr[p][:, :], start=True, stop=True),
                            sMM, 1)
                    on_t(ti)

                    def sn(s, i=i, p=p, Pn=Pn, is_q=is_q):
                        s.wait_ge(sMM, M[(i, "ind")])
                        if (i - 1, "rc") in M:
                            pass
                        s.drain()
                        s.activation(out=nrm_scr[0][0:2, :], in_=Pn[0:2, :],
                                     func=AF.Ln, bias=cv[0:2, CV_T24:CV_T24 + 1])
                        s.drain()
                        M[(i, "rc")] = inc(
                            s.activation(out=nrm2_scr[p][0:2, :],
                                         in_=nrm_scr[0][0:2, :], func=AF.Exp,
                                         scale=cv[0:2, CV_NHALF:CV_NHALF + 1],
                                         bias=cv[0:2, CV_LN8:CV_LN8 + 1]
                                         if is_q else
                                         cv[0:2, CV_Z2:CV_Z2 + 1]), sSC, 1)
                    on_s(sn)

                    def tb(t, i=i, p=p, Pn=Pn):
                        t.wait_ge(sSC, M[(i, "rc")])
                        M[(i, "bc")] = inc(
                            t.matmul(Pn[:, :], cINDT[:, :],
                                     nrm2_scr[p][0:2, :], start=True,
                                     stop=True), sMM, 1)
                    on_t(tb)

                    def se(s, i=i, p=p, Pn=Pn):
                        s.wait_ge(sMM, M[(i, "bc")])
                        if (i - 2, "vf") in M:
                            s.wait_ge(sVE, M[(i - 2, "vf")])
                        s.drain()
                        M[(i, "ev")] = inc(
                            s.activation(out=kq_scr[p][:, :], in_=Pn[:, :],
                                         func=AF.Copy), sSC, 1)
                    on_s(se)

                    def vf(v, i=i, p=p, Pp=Pp, ec=ec, qi=qi, tsl=tsl,
                           dst_fm=dst_fm):
                        v.wait_ge(sSC, M[(i, "ev")])
                        col = ec * T + qi * 1024 + tsl * 512
                        M[(i, "vf")] = inc(
                            v.tensor_tensor(out=dst_fm[:, col:col + 512],
                                            in0=kq_scr[p][:, :],
                                            in1=Pp[:, :], op=ALU.mult), sVE, 1)
                    on_v(vf)

    emit_kq(k_fm, wk_sb, False)
    emit_kq(q_fm, wq_sb, True)

    for qi in range(4):
        qm = [0]
        tv_base = val(sMM)
        vq_base = val(sVE)

        def gv(g, qi=qi, qm=qm):
            if kqstate["buffree"]:
                g.wait_ge(sMM, kqstate["buffree"])
            qm[0] = load_y2_quarter(g, qi)
        on_g(gv)

        tv_base = val(sMM)
        vq_base = val(sVE)
        vmm = lambda lt: tv_base + lt + 1
        vfr = lambda lt: vq_base + lt + 1

        def tvp(t, qi=qi, qm=qm, vmm=vmm, vfr=vfr):
            t.wait_ge(sY2, qm[0])
            t.wait_ge(sPW, pw_ld)
            if kqstate["marks"]:
                t.wait_ge(sVE, kqstate["marks"][(kqstate["idx"] - 1, "vf")])
            for lt in range(8):
                tt = qi * 8 + lt
                if lt >= 2:
                    t.wait_ge(sVE, vfr(lt - 2))
                elif qi >= 1 and kqstate.get("v_prev_ve"):
                    t.wait_ge(sVE, kqstate["v_prev_ve"] - 2 + lt + 1)
                pv = P[2 + tt % 2]
                ins = None
                for d in range(8):
                    lhs = y2buf[:, d * 1024 + lt * 128: d * 1024 + (lt + 1) * 128]
                    t.matmul(pv[:, :], lhs, wv_sb[:, d * E:(d + 1) * E],
                             start=(d == 0), stop=(d == 7))
                    ins = t.matmul(P[4][:, tt * 8:(tt + 1) * 8], lhs,
                                   wbeta_sb[:, d * H:(d + 1) * H],
                                   start=(d == 0), stop=(d == 7))
                inc(ins, sMM, 1)
            kqstate["buffree"] = val(sMM)
        on_t(tvp)
        assert val(sMM) == tv_base + 8

        def vvp(v, qi=qi, vmm=vmm):
            for lt in range(8):
                tt = qi * 8 + lt
                v.wait_ge(sMM, vmm(lt))
                inc(v.tensor_copy(out=v_tm[:, tt * E:(tt + 1) * E],
                                  in_=P[2 + tt % 2][:, :]), sVE, 1)
        on_v(vvp)
        assert val(sVE) == vq_base + 8
        kqstate["v_prev_ve"] = val(sVE)
    v_done = val(sVE)
    beta_mm = val(sMM)

    def sbeta(s):
        s.wait_ge(sMM, beta_mm)
        s.activation(out=beta_tm[:, :], in_=P[4][:, 0:256], func=AF.Sigmoid)
        s.drain()
        inc(s.activation(out=lr_tm[:, :], in_=beta_tm[:, :], func=AF.Ln,
                         scale=cv[:, CV_NEG1:CV_NEG1 + 1],
                         bias=cv[:, CV_1P:CV_1P + 1]), sSC, 1)
    on_s(sbeta)
    lr_done = val(sSC)

    def tcum(t):
        t.wait_ge(sSC, lr_done)
        for tt in range(NTT):
            if tt >= 1:
                t.wait_ge(sVE, cum_ve0 + tt)
            t.transpose(P[4][0:8, 0:128], lr_tm[:, tt * 8:(tt + 1) * 8],
                        cI128F[:, :])
            inc(t.transpose(P[5][0:8, 0:128], beta_tm[:, tt * 8:(tt + 1) * 8],
                            cI128F[:, :]), sMM, 1)
    tr_b = val(sMM)
    cum_ve0 = val(sVE)
    on_t(tcum)
    assert val(sMM) == tr_b + NTT

    def vcum(v):
        for tt in range(NTT):
            v.wait_ge(sMM, tr_b + tt + 1)
            v.tensor_copy(out=cum_hd[:, tt * 128:(tt + 1) * 128],
                          in_=P[4][0:8, 0:128])
            inc(v.tensor_copy(out=beta_hd[:, tt * 128:(tt + 1) * 128],
                              in_=P[5][0:8, 0:128]), sVE, 1)
        bufs_ = [cum_hd, cum_hd2]
        k = 1
        while k < T:
            a, b = bufs_[0], bufs_[1]
            v.drain()
            v.tensor_copy(out=b[:, 0:k], in_=a[:, 0:k])
            v.tensor_tensor(out=b[:, k:T], in0=a[:, k:T], in1=a[:, 0:T - k],
                            op=ALU.add)
            bufs_ = [b, a]
            k *= 2
        assert bufs_[0] is cum_hd
        v.drain()
        inc(v.tensor_copy(out=cum_hd[:, 0:1], in_=cum_hd[:, 0:1]), sVE, 1)
    on_v(vcum)
    cum_hdve = val(sVE)

    tr_b2 = val(sMM)

    def tcum2(t):
        t.wait_ge(sVE, cum_hdve)
        for tt in range(NTT):
            if tt >= 2:
                t.wait_ge(sVE, cum_hdve + tt - 1)
            inc(t.transpose(P[4 + tt % 2][0:128, 0:8],
                            cum_hd[:, tt * 128:(tt + 1) * 128],
                            cI128F[0:8, 0:8]), sMM, 1)
    on_t(tcum2)
    assert val(sMM) == tr_b2 + NTT

    def vcum2(v):
        for tt in range(NTT):
            v.wait_ge(sMM, tr_b2 + tt + 1)
            inc(v.tensor_copy(out=cum_tm[:, tt * 8:(tt + 1) * 8],
                              in_=P[4 + tt % 2][0:128, 0:8]), sVE, 1)
    on_v(vcum2)
    prep_ve = val(sVE)

    for nm, src in [("cum", cum_tm), ("beta", beta_tm)]:
        if nm in dbg:
            def dct(g, nm=nm, src=src):
                g.wait_ge(sVE, prep_ve)
                inc(g.dma_start(out=dbg[nm][:, :], in_=src[:, :]), sAT, 16)
            on_g(dct)
    if any(n in dbg for n in ("k", "q", "v")):
        def dkqv(g):
            g.wait_ge(sVE, prep_ve)
            if "k" in dbg:
                for ec in range(4):
                    inc(g.dma_start(out=dbg["k"][ec],
                                    in_=k_fm[:, ec * T:(ec + 1) * T]), sAT, 16)
            if "q" in dbg:
                for ec in range(4):
                    inc(g.dma_start(out=dbg["q"][ec],
                                    in_=q_fm[:, ec * T:(ec + 1) * T]), sAT, 16)
            if "v" in dbg:
                for tt in range(32):
                    inc(g.dma_start(out=dbg["v"][tt],
                                    in_=v_tm[:, tt * E:(tt + 1) * E]), sAT, 16)
        on_g(dkqv)

    esB2.close()

    esC = ExitStack()
    sbC = lambda n, s, d: esC.enter_context(nc.sbuf_tensor(n, s, d))
    attn_fm = sbC("attn_fm", [128, 4 * T], BF16)
    ct_sb = [sbC("ct%d" % i, [128, 512], F32) for i in range(2)]
    d_sb = [[sbC("d%d_%d" % (i, p), [128, 512], F32) for p in range(2)]
            for i in range(2)]
    bb_sb = [sbC("bb%d" % i, [128, 512], BF16) for i in range(2)]
    e_sb = [[sbC("e%d_%d" % (i, p), [128, 512], F32) for p in range(2)]
            for i in range(2)]
    w_sb = [[sbC("w%d_%d" % (i, p), [128, 512], BF16) for p in range(2)]
            for i in range(2)]

    prep_free = [prep_ve]
    for tr in range(8):
        t0 = tr * 512
        s_lo = tr * 4
        n_st = NTT - s_lo
        for hp in range(4):
            h0, h1 = 2 * hp, 2 * hp + 1
            ctb = val(sMM)
            sc_p = val(sSC)

            def tprep(t, t0=t0, h0=h0, h1=h1, ctb=ctb, sc_p=sc_p,
                      pf=prep_free[0]):
                t.wait_ge(sVE, pf)
                kk = 0
                for sel, src_hd in ((cSEL8, cum_hd), (cSEL8B, beta_hd)):
                    for h in (h0, h1):
                        if kk >= 2:
                            t.wait_ge(sSC, sc_p + kk - 1)
                        inc(t.matmul(P[4 + kk % 2][:, :],
                                     sel[:, h * 128:(h + 1) * 128],
                                     src_hd[0:8, t0: t0 + 512],
                                     start=True, stop=True), sMM, 1)
                        kk += 1
            on_t(tprep)
            assert val(sMM) == ctb + 4

            def sprep(s, ctb=ctb):
                kk = 0
                for dsts in (ct_sb, bb_sb):
                    for hh in range(2):
                        s.wait_ge(sMM, ctb + kk + 1)
                        s.drain()
                        inc(s.activation(out=dsts[hh][:, :],
                                         in_=P[4 + kk % 2][:, :],
                                         func=AF.Copy), sSC, 1)
                        kk += 1
            on_s(sprep)
            assert val(sSC) == sc_p + 4

            def vprep(v, sc4=val(sSC)):
                v.wait_ge(sSC, sc4)
                for hh in range(2):
                    v.drain()
                    inc(v.tensor_tensor(out=bb_sb[hh][:, :],
                                        in0=bb_sb[hh][:, :],
                                        in1=cIREP8[:, 0:512], op=ALU.mult),
                        sVE, 1)
            on_v(vprep)

            mm_base = val(sMM)
            sc_base = val(sSC)
            ve_base = val(sVE)

            def t_pre(si):
                return 0 if si == 0 else 2 + 4 * (si - 1)

            def kq_m(si, hh):
                return mm_base + t_pre(si) + (2 if si >= 1 else 0) + hh + 1

            def av_m(si, hh):
                if si < n_st - 1:
                    return mm_base + t_pre(si + 1) + hh + 1
                return mm_base + t_pre(n_st - 1) + 4 + hh + 1

            def e_m(si, hh):
                return sc_base + 2 * si + hh + 1

            def v_pre(si):
                return 0 if si == 0 else 4 * si - 2

            def d_m(si, hh):
                return ve_base + v_pre(si) + hh + 1

            def w_m(si, hh):
                if si < n_st - 1:
                    return ve_base + v_pre(si + 1) + 2 + hh + 1
                return ve_base + 4 * n_st - 2 + hh + 1

            def emit_av(t, si, hp, n_st):
                stp = ((tr * 4) + si)
                for hh in range(2):
                    t.wait_ge(sVE, w_m(si, hh))
                    po = P[2 + hh][64 * hh:64 * hh + 64, :]
                    inc(t.matmul(po,
                                 v_tm[:, stp * E + (2 * hp + hh) * DM:
                                      stp * E + (2 * hp + hh) * DM + 64],
                                 w_sb[hh][si % 2][:, :],
                                 start=(si == 0), stop=(si == n_st - 1),
                                 tile_position=(0, 64 * hh)), sMM, 1)

            def tmain(t, t0=t0, s_lo=s_lo, h0=h0, h1=h1, hp=hp, n_st=n_st,
                      emit_av=emit_av, bb_ve=ve_base):
                t.wait_ge(sVE, v_done)
                t.wait_ge(sVE, bb_ve)
                for si in range(n_st):
                    st = s_lo + si
                    if si >= 1:
                        emit_av(t, si - 1, hp, n_st)
                    band = si < 4
                    doff = si * 128
                    for hh, h in enumerate((h0, h1)):
                        ec = h // 2
                        off = 64 * (h % 2)
                        ins = t.matmul(
                            P[hh][:, :],
                            k_fm[off:off + 64, ec * T + st * 128: ec * T + (st + 1) * 128],
                            q_fm[off:off + 64, ec * T + t0: ec * T + t0 + 512],
                            start=True, stop=not band)
                        if band:
                            ins = t.matmul(
                                P[hh][:, doff:doff + 128], cI128B[:, :],
                                bb_sb[hh][:, doff:doff + 128],
                                start=False, stop=True)
                        inc(ins, sMM, 1)
                emit_av(t, n_st - 1, hp, n_st)
            on_t(tmain)
            assert val(sMM) == mm_base + 4 * n_st

            def smain(s, n_st=n_st, w_m=w_m, d_m=d_m):
                for si in range(n_st):
                    for hh in range(2):
                        s.wait_ge(sVE, d_m(si, hh))
                        if si >= 2:
                            s.wait_ge(sVE, w_m(si - 2, hh))
                        else:
                            s.drain()
                        inc(s.activation(out=e_sb[hh][si % 2][:, :],
                                         in_=d_sb[hh][si % 2][:, :],
                                         func=AF.Exp),
                            sSC, 1)
            on_s(smain)
            assert val(sSC) == sc_base + 2 * n_st

            def emit_w(v, si, n_st):
                off = si * 128
                for hh in range(2):
                    v.wait_ge(sSC, e_m(si, hh))
                    v.wait_ge(sMM, kq_m(si, hh))
                    wd = w_sb[hh][si % 2]
                    if si < 4:
                        ins = v.scalar_tensor_tensor(
                            out=wd[:, 0:off + 128],
                            in0=e_sb[hh][si % 2][:, 0:off + 128],
                            scalar=1.0, in1=P[hh][:, 0:off + 128],
                            op0=ALU.min, op1=ALU.mult)
                        v.drain()
                        ins = v.tensor_tensor(out=wd[:, off:off + 128],
                                              in0=wd[:, off:off + 128],
                                              in1=cTRILB[:, :], op=ALU.mult)
                    else:
                        ins = v.scalar_tensor_tensor(
                            out=wd[:, :], in0=e_sb[hh][si % 2][:, :],
                            scalar=1.0, in1=P[hh][:, :],
                            op0=ALU.min, op1=ALU.mult)
                    inc(ins, sVE, 1)

            def vmain(v, s_lo=s_lo, h0=h0, h1=h1, n_st=n_st, kq_m=kq_m,
                      e_m=e_m, emit_w=emit_w, pf=prep_free[0]):
                for hh in range(2):
                    for p in range(2):
                        v.memset(w_sb[hh][p][:, (p + 1) * 128:512], 0.0)
                for si in range(n_st):
                    st = s_lo + si
                    for hh, h in enumerate((h0, h1)):
                        if si >= 2:
                            v.wait_ge(sSC, e_m(si - 2, hh))
                        elif si == 0 and hh == 0:
                            v.drain()
                        inc(v.tensor_scalar(
                            out=d_sb[hh][si % 2][:, :], in0=ct_sb[hh][:, :],
                            scalar1=cum_tm[:, st * 8 + h: st * 8 + h + 1],
                            scalar2=0.0, op0=ALU.subtract, op1=ALU.min),
                            sVE, 1)
                    if si >= 1:
                        emit_w(v, si - 1, n_st)
                emit_w(v, n_st - 1, n_st)
            on_v(vmain)
            assert val(sVE) == ve_base + 4 * n_st

            av_last = val(sMM)

            def vav(v, hp=hp, t0=t0, av_last=av_last):
                v.wait_ge(sMM, av_last)
                v.tensor_copy(out=attn_fm[0:64, hp * T + t0: hp * T + t0 + 512],
                              in_=P[2][0:64, :])
                prep_free[0] = inc(v.tensor_copy(
                    out=attn_fm[64:128, hp * T + t0: hp * T + t0 + 512],
                    in_=P[3][64:128, :]), sVE, 1)
            on_v(vav)
    attn_ve = val(sVE)

    if "attnout" in dbg:
        def datt(g):
            g.wait_ge(sVE, attn_ve)
            for ec in range(4):
                inc(g.dma_start(out=dbg["attnout"][ec],
                                in_=attn_fm[:, ec * T:(ec + 1) * T]), sAT, 16)
        on_g(datt)

    def ph5(g):
        g.wait_ge(sVE, attn_ve)
        for ec in range(4):
            inc(g.dma_start(out=cc2_in_t.ap()[ec],
                            in_=attn_fm[:, ec * T:(ec + 1) * T]), sCC2, 16)
        g.wait_ge(sCC2, val(sCC2))
        i = g.collective_compute(
            "AllGather", ALU.bypass,
            replica_groups=[[0, 1], [2, 3], [4, 5], [6, 7]],
            ins=[cc2_in_t.ap().opt()], outs=[cc2_out_t.ap().opt()])
        inc(i, sCL, 1)
    on_g(ph5)
    a2a_done = val(sCL)
    esC.close()
    esB1.close()

    esD = ExitStack()
    sbD = lambda n, s, d: esD.enter_context(nc.sbuf_tensor(n, s, d))
    attnA = sbD("attnA", [128, 4 * TOK], BF16)
    attnB = sbD("attnB", [128, 4 * TOK], BF16)
    wout_sb = sbD("wout_sb", [128, 8 * D], BF16)
    xm_t = [sbD("xm%d" % i, [128, 1024], F32) for i in range(2)]
    x2_t = [sbD("x2%d" % i, [128, 1024], F32) for i in range(2)]

    def ph6g(g):
        g.wait_ge(sCL, a2a_done)
        pid = g.partition_id()
        off = (pid % 2) * TOK
        for ec in range(4):
            inc(g.dma_start(out=attnA[:, ec * TOK:(ec + 1) * TOK],
                            in_=cc2_out_t.ap()[0, ec][:, ds(off, TOK)]),
                sAT, 16)
            inc(g.dma_start(out=attnB[:, ec * TOK:(ec + 1) * TOK],
                            in_=cc2_out_t.ap()[1, ec][:, ds(off, TOK)]),
                sAT, 16)
        inc(g.dma_start(out=wout_sb[:, :],
                        in_=wout_d.rearrange("p c e -> p (c e)")), sAT, 16)
    on_g(ph6g)
    a2a_ld = val(sAT)

    if "attnfull" in dbg:
        def datf(g):
            g.wait_ge(sAT, a2a_ld)
            for ec in range(4):
                inc(g.dma_start(out=dbg["attnfull"][ec],
                                in_=attnA[:, ec * TOK:(ec + 1) * TOK]), sAT, 16)
                inc(g.dma_start(out=dbg["attnfull"][4 + ec],
                                in_=attnB[:, ec * TOK:(ec + 1) * TOK]), sAT, 16)
        on_g(datf)

    xm_marks, wo_marks, x2v_marks, x2st_marks = {}, {}, {}, {}
    ph6_st_gate = val(sST)
    for tt in range(16):
        def ywo(y, tt=tt):
            if tt == 0:
                y.wait_ge(sST, ph6_st_gate)
                y.wait_ge(sAT, a2a_ld)
            if tt >= 2:
                y.wait_ge(sVE, x2v_marks[tt - 2])
            xm_marks[tt] = inc(y.dma_start(
                out=xm_t[tt % 2][:, :],
                in_=xmid_hbm[tt * 128:(tt + 1) * 128, :]), sXM[tt % 2], 16)
        on_y(ywo)

        def two(t, tt=tt):
            t.wait_ge(sAT, a2a_ld)
            if tt >= 2:
                t.wait_ge(sVE, x2v_marks[tt - 2])
            pa, pb = P[2 * (tt % 2)], P[2 * (tt % 2) + 1]
            for ec in range(4):
                lhsA = attnA[:, ec * TOK + tt * 128: ec * TOK + (tt + 1) * 128]
                t.matmul(pa[:, :], lhsA, wout_sb[:, ec * D: ec * D + 512],
                         start=(ec == 0), stop=False)
                t.matmul(pb[:, :], lhsA, wout_sb[:, ec * D + 512:(ec + 1) * D],
                         start=(ec == 0), stop=False)
            i2 = None
            for ec in range(4):
                lhsB = attnB[:, ec * TOK + tt * 128: ec * TOK + (tt + 1) * 128]
                t.matmul(pa[:, :], lhsB,
                         wout_sb[:, (4 + ec) * D: (4 + ec) * D + 512],
                         start=False, stop=(ec == 3))
                i2 = t.matmul(pb[:, :], lhsB,
                              wout_sb[:, (4 + ec) * D + 512:(5 + ec) * D],
                              start=False, stop=(ec == 3))
            wo_marks[tt] = inc(i2, sMM, 1)
        on_t(two)

        def vwo(v, tt=tt):
            v.wait_ge(sMM, wo_marks[tt])
            v.wait_ge(sXM[tt % 2], xm_marks[tt])
            if tt >= 2:
                v.wait_ge(sX2[tt % 2], x2st_marks[tt - 2])
            pa, pb = P[2 * (tt % 2)], P[2 * (tt % 2) + 1]
            v.tensor_tensor(out=x2_t[tt % 2][:, 0:512],
                            in0=xm_t[tt % 2][:, 0:512], in1=pa[:, :],
                            op=ALU.add)
            x2v_marks[tt] = inc(v.tensor_tensor(
                out=x2_t[tt % 2][:, 512:1024],
                in0=xm_t[tt % 2][:, 512:1024], in1=pb[:, :], op=ALU.add),
                sVE, 1)
        on_v(vwo)

        def yx2(y, tt=tt):
            y.wait_ge(sVE, x2v_marks[tt])
            x2st_marks[tt] = inc(y.dma_start(
                out=x2_hbm[tt * 128:(tt + 1) * 128, :],
                in_=x2_t[tt % 2][:, :]), sX2[tt % 2], 16)
            if "x2" in dbg:
                inc(y.dma_start(out=dbg["x2"][tt * 128:(tt + 1) * 128, :],
                                in_=x2_t[tt % 2][:, :]), sST, 16)
        on_y(yx2)
    x2_final = (val(sX2[0]), val(sX2[1]))
    esD.close()

    esE = ExitStack()
    sbE = lambda n, s, d: esE.enter_context(nc.sbuf_tensor(n, s, d))
    bufs = {
        "lnsrc": [sbE("lnsrc3%d" % i, [128, NTI * 1024], F32) for i in range(2)],
        "xout": sbE("xout3", [128, NTI * 1024], F32),
        "yf": [sbE("yf3%d" % i, [128, 8 * TC], BF16) for i in range(2)],
        "ytile": [sbE("ytileE%d" % i, [128, 1024], BF16) for i in range(4)],
        "sil": [sbE("silE%d" % i, [128, 512], F32) for i in range(2)],
        "ffo": [sbE("ffoE%d" % i, [128, 512], F32) for i in range(2)],
        "h_fm": sbE("h_fmE", [128, 32 * TC], BF16),
        "h2_fm": sbE("h2_fmE", [128, 32 * TC], BF16),
        "wbuf": [sbE("wbufE%d" % i, [128, 4096], BF16) for i in range(4)],
        "sq_ln": sbE("sq_lnE", [128, 1024], F32),
    }

    hist3 = {}

    def yx3_ck(ck):
        def yx3(y, ck=ck):
            if ck == 0:
                y.wait_ge(sX2[0], x2_final[0])
                y.wait_ge(sX2[1], x2_final[1])
            if ck >= 2:
                y.wait_ge(sVE, hist3[(ck - 2, "out_ve")])
            for i in range(NTI):
                inc(y.dma_start(
                    out=bufs["lnsrc"][ck % 2][:, i * 1024:(i + 1) * 1024],
                    in_=x2_hbm[ck * TC + i * 128: ck * TC + (i + 1) * 128, :]),
                    sXp[ck % 2], 16)
        on_y(yx3)
        return (sXp[ck % 2], val(sXp[ck % 2]))

    xld0 = yx3_ck(0)
    hist3[(0, "xld")] = xld0
    hist3[(0, "lst")] = emit_ln_stats(bufs["lnsrc"][0], 0, xld0)
    hist3[(0, "ln_ve")] = emit_ln_finish(hist3[(0, "lst")], bufs["yf"][0], 0, 0)
    for ck in range(NCHUNK):
        par = ck % 2
        lnsrc = bufs["lnsrc"][par]
        yf = bufs["yf"][par]

        def midcb3(ck=ck):
            if ck + 1 < NCHUNK:
                xld = yx3_ck(ck + 1)
                hist3[(ck + 1, "xld")] = xld
                hist3[(ck + 1, "lst")] = emit_ln_stats(
                    bufs["lnsrc"][(ck + 1) % 2], 0, xld)
            return None

        st_mark, out_ve = emit_ffn(
            yf, hist3[(ck, "ln_ve")], lnsrc, 0, hist3.get((ck, "xld"), 0),
            bufs["xout"],
            0, (sST, hist3[(ck - 1, "store")]) if ck >= 1 else 0,
            y_out, ck, False, None, mid_cb=midcb3)
        hist3[(ck, "t12_end")] = ffn_state["t12_end"]
        hist3[(ck, "out_ve")] = out_ve
        hist3[(ck, "store")] = st_mark
        if ck + 1 < NCHUNK:
            hist3[(ck + 1, "ln_ve")] = emit_ln_finish(
                hist3[(ck + 1, "lst")], bufs["yf"][(ck + 1) % 2],
                hist3.get((ck - 1, "t12_end"), 0), 0)

    finals_y = [(sST, val(sST)), (sXp[0], val(sXp[0])),
                (sXp[1], val(sXp[1])), (sC, val(sC)),
                (sX2[0], val(sX2[0])), (sX2[1], val(sX2[1])),
                (sXM[0], val(sXM[0])), (sXM[1], val(sXM[1]))]
    finals_g = [(sAT, val(sAT)), (sY2, val(sY2)),
                (sCCp[0], val(sCCp[0])), (sCCp[1], val(sCCp[1])),
                (sCC2, val(sCC2)),
                (sPW, val(sPW))] + [(sW[i], val(sW[i])) for i in range(4)] + \
               [(sCW[i], val(sCW[i])) for i in range(4)] +                [(sXM[i], val(sXM[i])) for i in range(2)]

    def yfin(y):
        for s, v in finals_y:
            if v:
                y.wait_ge(s, v)
    on_y(yfin)

    def gfin(g):
        for s, v in finals_g:
            if v:
                g.wait_ge(s, v)
    on_g(gfin)

    esE.close()
    es0.close()
    return nc


_CACHED = {}


def kernel(**inputs):
    in_maps = host_prepare(inputs)
    if "nc" not in _CACHED:
        _CACHED["nc"] = build_nc()
    nc = _CACHED["nc"]
    res = run_bass_kernel_spmd(nc, in_maps, core_ids=list(range(8)))
    out = np.empty((B_, T, D), dtype=np.float32)
    for c in range(8):
        b, j = c // 2, c % 2
        out[b, j * TOK:(j + 1) * TOK, :] = res.results[c]["y_out"]
    return out


# version 08
# speedup vs baseline: 1.0759x; 1.0759x over previous
import numpy as np
from contextlib import ExitStack

import concourse.bass as bass
import concourse.mybir as mybir
from concourse.bass_utils import run_bass_kernel_spmd
from concourse.bass import ds

F32 = mybir.dt.float32
BF16 = mybir.dt.bfloat16
AF = mybir.ActivationFunctionType
ALU = mybir.AluOpType
AX = mybir.AxisListType.X

B_, T, D = 4, 4096, 1024
HID = 4096
TOK = 2048
TC = 512
NCHUNK = TOK // TC
NTI = TC // 128
H = 8
DM = 64
E = H * DM
NTT = T // 128
EPS = 1e-5

DEBUG_OUT = []


def host_prepare(inputs):
    asc = np.ascontiguousarray
    x = np.asarray(inputs["L0_attn_x"], dtype=np.float32)
    Wk = np.asarray(inputs["L0_attn_Wk"], np.float32)
    Wv = np.asarray(inputs["L0_attn_Wv"], np.float32)
    Wq = np.asarray(inputs["L0_attn_Wq"], np.float32)
    Wb = np.asarray(inputs["L0_attn_Wbeta"], np.float32)
    Wo = np.asarray(inputs["L0_attn_Wout"], np.float32)

    def tile_w(wT, kdim, odim, ocols):
        nk, nob = kdim // 128, odim // ocols
        return asc(wT.reshape(nk, 128, nob, ocols).transpose(2, 1, 0, 3))

    def pmajor(w):
        k, n = w.shape
        return asc(w.reshape(k // 128, 128, n).transpose(1, 0, 2))

    w1T = np.asarray(inputs["L0_ffn_w1_w"], np.float32).T
    w2T = np.asarray(inputs["L0_ffn_w2_w"], np.float32).T
    h1T = np.asarray(inputs["L0_ffn_h1_w"], np.float32).T
    h2T = np.asarray(inputs["L0_ffn_h2_w"], np.float32).T
    w3T = np.asarray(inputs["L0_ffn_w3_w"], np.float32).T

    shared = {
        "w1t": tile_w(w1T, D, HID, 512),
        "w2t": tile_w(w2T, D, HID, 512),
        "h1t": tile_w(h1T, HID, HID, 128),
        "h2t": tile_w(h2T, HID, HID, 128),
        "w3t": tile_w(w3T, HID, D, 128),
        "s1pc": asc(np.asarray(inputs["L0_ffn_w1_s"], np.float32).reshape(32, 128).T),
        "s2pc": asc(np.asarray(inputs["L0_ffn_w2_s"], np.float32).reshape(32, 128).T),
        "h1spc": asc(np.asarray(inputs["L0_ffn_h1_s"], np.float32).reshape(32, 128).T),
        "h2spc": asc(np.asarray(inputs["L0_ffn_h2_s"], np.float32).reshape(32, 128).T),
        "s3pc": asc(np.asarray(inputs["L0_ffn_w3_s"], np.float32).reshape(8, 128).T),
        "wout": pmajor(Wo),
    }
    in_maps = []
    for c in range(8):
        b, j = c // 2, c % 2
        m = dict(shared)
        m["x_own"] = asc(x[b, j * TOK:(j + 1) * TOK, :])
        m["wk"] = pmajor(asc(Wk[:, j * E:(j + 1) * E]))
        m["wv"] = pmajor(asc(Wv[:, j * E:(j + 1) * E]))
        m["wq"] = pmajor(asc(Wq[:, j * E:(j + 1) * E]))
        m["wbeta"] = pmajor(asc(Wb[:, j * H:(j + 1) * H]))
        in_maps.append(m)
    return in_maps


def build_nc():
    nc = bass.Bass("TRN2", target_bir_lowering=False, debug=False)
    ap = lambda t: t.ap()

    x_own = ap(nc.dram_tensor("x_own", [TOK, D], F32, kind="ExternalInput"))
    wk_d = ap(nc.dram_tensor("wk", [128, 8, E], F32, kind="ExternalInput"))
    wv_d = ap(nc.dram_tensor("wv", [128, 8, E], F32, kind="ExternalInput"))
    wq_d = ap(nc.dram_tensor("wq", [128, 8, E], F32, kind="ExternalInput"))
    wbeta_d = ap(nc.dram_tensor("wbeta", [128, 8, H], F32, kind="ExternalInput"))
    wout_d = ap(nc.dram_tensor("wout", [128, 8, D], F32, kind="ExternalInput"))
    w1t = ap(nc.dram_tensor("w1t", [8, 128, 8, 512], F32, kind="ExternalInput"))
    w2t = ap(nc.dram_tensor("w2t", [8, 128, 8, 512], F32, kind="ExternalInput"))
    h1t = ap(nc.dram_tensor("h1t", [32, 128, 32, 128], F32, kind="ExternalInput"))
    h2t = ap(nc.dram_tensor("h2t", [32, 128, 32, 128], F32, kind="ExternalInput"))
    w3t = ap(nc.dram_tensor("w3t", [8, 128, 32, 128], F32, kind="ExternalInput"))
    s1pc = ap(nc.dram_tensor("s1pc", [128, 32], F32, kind="ExternalInput"))
    s2pc = ap(nc.dram_tensor("s2pc", [128, 32], F32, kind="ExternalInput"))
    h1spc = ap(nc.dram_tensor("h1spc", [128, 32], F32, kind="ExternalInput"))
    h2spc = ap(nc.dram_tensor("h2spc", [128, 32], F32, kind="ExternalInput"))
    s3pc = ap(nc.dram_tensor("s3pc", [128, 8], F32, kind="ExternalInput"))
    y_out = ap(nc.dram_tensor("y_out", [TOK, D], F32, kind="ExternalOutput"))

    xmid_hbm = ap(nc.dram_tensor("xmid_hbm", [TOK, D], F32, kind="Internal"))
    x2_hbm = ap(nc.dram_tensor("x2_hbm", [TOK, D], F32, kind="Internal"))
    w1c = ap(nc.dram_tensor("w1c", [8, 128, 8, 512], BF16, kind="Internal"))
    w2c = ap(nc.dram_tensor("w2c", [8, 128, 8, 512], BF16, kind="Internal"))
    h1c = ap(nc.dram_tensor("h1c", [32, 128, 32, 128], BF16, kind="Internal"))
    h2c = ap(nc.dram_tensor("h2c", [32, 128, 32, 128], BF16, kind="Internal"))
    w3c = ap(nc.dram_tensor("w3c", [8, 128, 32, 128], BF16, kind="Internal"))
    cc1_in_t = [nc.dram_tensor("cc1_in%d" % i, [8, 128, TOK // 2], BF16,
                               kind="Internal") for i in range(2)]
    cc1_out_t = [nc.dram_tensor("cc1_out%d" % i, [2, 8, 128, TOK // 2], BF16,
                                kind="Internal") for i in range(2)]
    cc2_in_t = nc.dram_tensor("cc2_in", [4, 128, T], BF16, kind="Internal")
    cc2_out_t = nc.dram_tensor("cc2_out", [2, 4, 128, T], BF16, kind="Internal")

    dbg = {}
    for name in DEBUG_OUT:
        shp = {"xmid": [TOK, D], "x2": [TOK, D], "y2": [8, 128, T],
               "k": [4, 128, T], "q": [4, 128, T], "v": [32, 128, E],
               "cum": [128, 256], "beta": [128, 256],
               "attnout": [4, 128, T], "attnfull": [8, 128, TOK]}[name]
        dbg[name] = ap(nc.dram_tensor("dbg_" + name, shp, F32,
                                      kind="ExternalOutput"))

    i128f = np.eye(128, dtype=np.float32)
    consts = {
        "I128F": i128f, "I128B": i128f,
        "TRILB": np.tril(np.ones((128, 128), np.float32)),
        "TRIUF": np.triu(np.ones((128, 128), np.float32)),
        "ONESF": np.ones((128, 128), np.float32),
        "ONES1": np.ones((1, 128), np.float32),
        "SEL8": np.stack([np.repeat(np.eye(8, dtype=np.float32)[:, h:h+1], 128, 1)
                          for h in range(8)], 0).transpose(1, 0, 2).reshape(8, 1024),
        "IREP8": np.tile(i128f, (1, 8)),
        "IND64": np.stack([np.r_[np.ones(64), np.zeros(64)],
                           np.r_[np.zeros(64), np.ones(64)]], 1).astype(np.float32),
        "INDT": np.stack([np.r_[np.ones(64), np.zeros(64)],
                          np.r_[np.zeros(64), np.ones(64)]], 0).astype(np.float32),
        "CVALS": np.tile(np.array([[EPS, 1.0 + 1e-8, 1.0 / D, 0.0,
                                    1e-12, 0.5, -1.0, 1.0,
                                    -0.5, np.log(0.125), 1e-24, 0.0]],
                                   np.float32), (128, 1)),
    }
    cd = {k: ap(nc.inline_tensor(v.astype(np.float32), name="c_" + k))
          for k, v in consts.items()}

    es0 = ExitStack()
    sb0 = lambda n, s, d: es0.enter_context(nc.sbuf_tensor(n, s, d))
    P = [es0.enter_context(nc.psum_tensor("P%d" % i, [128, 512], F32))
         for i in range(6)]
    PB = [es0.enter_context(nc.psum_tensor("PB%d" % i, [128, 1024], BF16))
          for i in range(2)]

    cI128F = sb0("cI128F", [128, 128], F32)
    cI128B = sb0("cI128B", [128, 128], BF16)
    cTRILB = sb0("cTRILB", [128, 128], BF16)
    cTRIUF = sb0("cTRIUF", [128, 128], F32)
    cONESF = sb0("cONESF", [128, 128], F32)
    cONES1 = sb0("cONES1", [1, 128], F32)
    cSEL8 = sb0("cSEL8", [8, 1024], F32)
    cSEL8B = sb0("cSEL8B", [8, 1024], BF16)
    cIREP8 = sb0("cIREP8", [128, 1024], BF16)
    cIND64 = sb0("cIND64", [128, 2], F32)
    cINDT = sb0("cINDT", [2, 128], F32)
    cv = sb0("cv", [128, 12], F32)
    (CV_EPS, CV_1P, CV_INVD, CV_Z, CV_1E12, CV_HALF, CV_NEG1, CV_ONE,
     CV_NHALF, CV_LN8, CV_T24, CV_Z2) = range(12)
    s1_sb = sb0("s1_sb", [128, 32], F32)
    s2_sb = sb0("s2_sb", [128, 32], F32)
    h1s_sb = sb0("h1s_sb", [128, 32], F32)
    h2s_sb = sb0("h2s_sb", [128, 32], F32)
    s3h_sb = sb0("s3h_sb", [128, 8], F32)
    stats = sb0("stats", [128, 64], F32)
    beta_tm = sb0("beta_tm", [128, 256], F32)
    lr_tm = sb0("lr_tm", [128, 256], F32)
    cum_tm = sb0("cum_tm", [128, 256], F32)
    ncum_tm = sb0("ncum_tm", [128, 256], F32)

    sem_names = ["sMM", "sVE", "sSC", "sCC", "sCCB", "sCC2", "sCL", "sC", "sX", "sST",
                 "sCW0", "sCW1", "sCW2", "sCW3",
                 "sY2", "sPW", "sAT", "sXM0", "sXM1", "sX2A", "sX2B",
                 "sW0", "sW1", "sW2", "sW3"]
    sems = {n: es0.enter_context(nc.semaphore(n)) for n in sem_names}
    sMM, sVE, sSC, sCC = (sems[k] for k in ["sMM", "sVE", "sSC", "sCC"])
    sCL = sems["sCL"]
    sCCp = [sems["sCC"], sems["sCCB"]]
    sCC2 = sems["sCC2"]
    sC, sX, sST = (sems[k] for k in ["sC", "sX", "sST"])
    sCW = [sems["sCW%d" % i] for i in range(4)]
    sY2, sPW, sAT = (sems[k] for k in ["sY2", "sPW", "sAT"])
    sXM = [sems["sXM0"], sems["sXM1"]]
    sX2 = [sems["sX2A"], sems["sX2B"]]
    sW = [sems["sW%d" % i] for i in range(4)]
    cnt = {n: 0 for n in sem_names}

    def inc(instr, sem, k=1):
        cnt[sem.name] += k
        instr.then_inc(sem, k)
        return cnt[sem.name]

    def val(sem):
        return cnt[sem.name]

    block = es0.enter_context(nc.Block())

    def on_t(fn):
        @block.tensor
        def _(t):
            fn(t)

    def on_v(fn):
        @block.vector
        def _(v):
            fn(v)

    def on_s(fn):
        @block.scalar
        def _(s):
            fn(s)

    def on_g(fn):
        @block.gpsimd
        def _(g):
            fn(g)

    def on_y(fn):
        @block.sync
        def _(y):
            fn(y)

    def ph0y(y):
        for name, tile in [("I128F", cI128F), ("TRIUF", cTRIUF),
                           ("ONESF", cONESF), ("IND64", cIND64),
                           ("INDT", cINDT), ("CVALS", cv), ("ONES1", cONES1),
                           ("SEL8", cSEL8)]:
            inc(y.dma_start(out=tile[:, :], in_=cd[name][:, :]), sC, 16)
        for dst, src in [(s1_sb, s1pc), (s2_sb, s2pc), (h1s_sb, h1spc),
                         (h2s_sb, h2spc), (s3h_sb, s3pc)]:
            inc(y.dma_start(out=dst[:, :], in_=src[:, :]), sC, 16)
    on_y(ph0y)

    def ph0g(g):
        for name, tile in [("I128B", cI128B), ("TRILB", cTRILB),
                           ("IREP8", cIREP8), ("SEL8", cSEL8B)]:
            inc(g.dma_start(out=tile[:, :], in_=cd[name][:, :]), sPW, 16)
    on_g(ph0g)
    ph0_ld, ph0_wt = val(sC), val(sPW)

    def ph0s(s):
        s.wait_ge(sC, ph0_ld)
        i = s.activation(out=s3h_sb[:, :], in_=s3h_sb[:, :], func=AF.Copy,
                         scale=cv[:, CV_HALF:CV_HALF + 1])
        inc(i, sSC, 1)
    on_s(ph0s)
    s3h_ready = val(sSC)

    def ph0t(t):
        t.wait_ge(sC, ph0_ld)
        t.wait_ge(sPW, ph0_wt)
    on_t(ph0t)

    bufs = {}
    ln_state = {}

    def emit_ln(src_tm, ready_ve, ready_ld, dst_fm, dst_free_mm, dst_free_cc):
        ytile, sq_ln = bufs["ytile"], bufs["sq_ln"]
        mm_base = val(sMM)
        sc0 = None

        def vst1(v):
            if ready_ve:
                v.wait_ge(sVE, ready_ve)
            if ready_ld:
                v.wait_ge(ready_ld[0], ready_ld[1])
            ins = None
            for i in range(NTI):
                xt = src_tm[:, i * 1024:(i + 1) * 1024]
                st = lambda c: stats[:, i * 8 + c:i * 8 + c + 1]
                v.reduce_sum(out=st(0), in_=xt, axis=AX)
                v.drain()
                ins = v.tensor_scalar(out=st(2), in0=st(0), scalar1=-1.0 / D,
                                      scalar2=None, op0=ALU.mult)
                v.drain()
            inc(ins, sVE, 1)
        on_v(vst1)
        v1 = val(sVE)

        def sst1(s):
            if ready_ve:
                s.wait_ge(sVE, ready_ve)
            if ready_ld:
                s.wait_ge(ready_ld[0], ready_ld[1])
            ins = None
            for i in range(NTI):
                s.drain()
                ins = s.activation(out=sq_ln[:, :],
                                   in_=src_tm[:, i * 1024:(i + 1) * 1024],
                                   func=AF.Square,
                                   accum_out=stats[:, i * 8 + 1:i * 8 + 2])
            inc(ins, sSC, 1)
        on_s(sst1)
        s1m = val(sSC)

        def vst2(v):
            v.wait_ge(sSC, s1m)
            v.wait_ge(sVE, v1)
            ins = None
            for i in range(NTI):
                st = lambda c: stats[:, i * 8 + c:i * 8 + c + 1]
                v.drain()
                v.tensor_tensor(out=st(3), in0=st(2), in1=st(2), op=ALU.mult)
                v.drain()
                ins = v.scalar_tensor_tensor(out=st(4), in0=st(1),
                                             scalar=1.0 / D, in1=st(3),
                                             op0=ALU.mult, op1=ALU.subtract)
            v.drain()
            inc(ins, sVE, 1)
        on_v(vst2)
        v2 = val(sVE)

        def sst2(s):
            s.wait_ge(sVE, v2)
            ins = None
            for i in range(NTI):
                st = lambda c: stats[:, i * 8 + c:i * 8 + c + 1]
                s.drain()
                ins = s.activation(out=st(5), in_=st(4), func=AF.Sqrt,
                                   bias=cv[:, CV_EPS:CV_EPS + 1])
            inc(ins, sSC, 1)
        on_s(sst2)
        s2m = val(sSC)

        def vst3(v):
            v.wait_ge(sSC, s2m)
            ins = None
            for i in range(NTI):
                st = lambda c: stats[:, i * 8 + c:i * 8 + c + 1]
                v.drain()
                v.reciprocal(out=st(6), in_=st(5))
                v.drain()
                ins = v.tensor_tensor(out=st(7), in0=st(2), in1=st(6),
                                      op=ALU.mult)
            v.drain()
            inc(ins, sVE, 1)
        on_v(vst3)
        v3 = val(sVE)

        sc0 = val(sVE)

        def vst4(v):
            pb = ln_state.get("prev_mm_base")
            for i in range(NTI):
                st = lambda c: stats[:, i * 8 + c:i * 8 + c + 1]
                if pb is not None:
                    v.wait_ge(sMM, pb + i + 1)
                v.drain()
                ins = v.tensor_scalar(out=ytile[i][:, :],
                                      in0=src_tm[:, i * 1024:(i + 1) * 1024],
                                      scalar1=st(6), scalar2=st(7),
                                      op0=ALU.mult, op1=ALU.add)
                inc(ins, sVE, 1)
        on_v(vst4)
        ln_state["prev_mm_base"] = mm_base
        assert val(sVE) == sc0 + NTI

        ve_base = val(sVE)

        def tst(t):
            t.wait_ge(sVE, sc0)
            for i in range(NTI):
                t.wait_ge(sVE, sc0 + i + 1)
                if i >= 1:
                    t.wait_ge(sVE, ve_base + 8 * i)
                ins = None
                for dc in range(8):
                    pb = PB[dc // 4]
                    ins = t.transpose(pb[:, (dc % 4) * 128:(dc % 4) * 128 + 128],
                                      ytile[i][:, dc * 128:(dc + 1) * 128],
                                      cI128B[:, :])
                inc(ins, sMM, 1)
        on_t(tst)
        assert val(sMM) == mm_base + NTI

        def vtr(v):
            if dst_free_mm:
                v.wait_ge(sMM, dst_free_mm)
            if dst_free_cc:
                v.wait_ge(dst_free_cc[0], dst_free_cc[1])
            for i in range(NTI):
                v.wait_ge(sMM, mm_base + i + 1)
                for dc in range(8):
                    pb = PB[dc // 4]
                    ins = v.tensor_copy(
                        out=dst_fm[:, dc * TC + i * 128: dc * TC + (i + 1) * 128],
                        in_=pb[:, (dc % 4) * 128:(dc % 4) * 128 + 128])
                    inc(ins, sVE, 1)
        on_v(vtr)
        assert val(sVE) == ve_base + 8 * NTI
        return val(sVE)

    ffn_state = {"prev_mm_end": 0, "cache_wt": 0}
    cwcnt = [0, 0, 0, 0]

    def emit_ffn(y_fm, y_ready_ve, res_tm, res_ready_ve, res_ready_ld,
                 out_tm, out_free_ve, out_free_ld, out_hbm, ck, build_cache,
                 dbg_store):
        sil, ffo, h_fm, h2_fm, wbuf = (bufs["sil"], bufs["ffo"],
                                       bufs["h_fm"], bufs["h2_fm"],
                                       bufs["wbuf"])

        t12_base = val(sMM)
        mm_w1 = lambda oc: t12_base + 2 * oc + 1
        mm_w2 = lambda oc: t12_base + 2 * oc + 2
        v12_base = val(sVE)
        ve_h = lambda oc: v12_base + oc + 1
        s12_base = val(sSC)
        sc_sil = lambda oc: s12_base + oc + 1
        prev_end = ffn_state["prev_mm_end"]

        w12m = []
        cw_jobs = []

        def g12(g):
            if not build_cache and ffn_state["cache_wt"]:
                for b in range(4):
                    g.wait_ge(sCW[b], ffn_state["cache_wt"][b])
            for ob in range(8):
                b0, b1 = 2 * (ob % 2), 2 * (ob % 2) + 1
                if ob < 2:
                    if prev_end:
                        g.wait_ge(sMM, prev_end)
                    if build_cache:
                        for b in (b0, b1):
                            if cwcnt[b]:
                                g.wait_ge(sCW[b], cwcnt[b])
                else:
                    g.wait_ge(sMM, mm_w2(4 * (ob - 2) + 3))
                    if build_cache:
                        g.wait_ge(sCW[b0], cwcnt[b0])
                        g.wait_ge(sCW[b1], cwcnt[b1])
                src1 = (w1t if build_cache else w1c)[ob]
                src2 = (w2t if build_cache else w2c)[ob]
                m1 = inc(g.dma_start(out=wbuf[b0][:, :],
                                     in_=src1.rearrange("p k c -> p (k c)")),
                         sW[b0], 16)
                m2 = inc(g.dma_start(out=wbuf[b1][:, :],
                                     in_=src2.rearrange("p k c -> p (k c)")),
                         sW[b1], 16)
                w12m.append((m1, m2))
                if build_cache:
                    for b, m, wc in ((b0, m1, w1c[ob]), (b1, m2, w2c[ob])):
                        cwcnt[b] += 16
                        cw_jobs.append((sW[b], m, wbuf[b][:, :],
                                        wc.rearrange("p k c -> p (k c)"),
                                        sCW[b]))
        on_g(g12)
        if build_cache:
            def ycw12(y, jobs=list(cw_jobs)):
                for sw, m, srcap, dstap, scw in jobs:
                    y.wait_ge(sw, m)
                    inc(y.dma_start(out=dstap, in_=srcap), scw, 16)
            on_y(ycw12)

        def t12(t):
            t.wait_ge(sVE, y_ready_ve)
            for oc in range(32):
                ob, sub = oc // 4, oc % 4
                t.wait_ge(sW[2 * (ob % 2)], w12m[ob][0])
                t.wait_ge(sW[2 * (ob % 2) + 1], w12m[ob][1])
                if oc >= 2:
                    t.wait_ge(sVE, ve_h(oc - 2))
                pa, pb = P[2 * (oc % 2)], P[2 * (oc % 2) + 1]
                ins = None
                for k in range(8):
                    ins = t.matmul(pa[:, :],
                                   wbuf[2 * (ob % 2)][:, k * 512 + sub * 128:
                                                      k * 512 + sub * 128 + 128],
                                   y_fm[:, k * TC:(k + 1) * TC],
                                   start=(k == 0), stop=(k == 7))
                inc(ins, sMM, 1)
                for k in range(8):
                    ins = t.matmul(pb[:, :],
                                   wbuf[2 * (ob % 2) + 1][:, k * 512 + sub * 128:
                                                          k * 512 + sub * 128 + 128],
                                   y_fm[:, k * TC:(k + 1) * TC],
                                   start=(k == 0), stop=(k == 7))
                inc(ins, sMM, 1)
        on_t(t12)
        assert val(sMM) == t12_base + 64

        def s12(s):
            for oc in range(32):
                s.wait_ge(sMM, mm_w1(oc))
                if oc >= 2:
                    s.wait_ge(sVE, ve_h(oc - 2))
                ins = s.activation(out=sil[oc % 2][:, :],
                                   in_=P[2 * (oc % 2)][:, :],
                                   func=AF.Silu, scale=s1_sb[:, oc:oc + 1])
                inc(ins, sSC, 1)
        on_s(s12)
        assert val(sSC) == s12_base + 32

        def v12(v):
            for oc in range(32):
                v.wait_ge(sMM, mm_w2(oc))
                v.wait_ge(sSC, sc_sil(oc))
                ins = v.scalar_tensor_tensor(
                    out=h_fm[:, oc * TC:(oc + 1) * TC],
                    in0=P[2 * (oc % 2) + 1][:, :],
                    scalar=s2_sb[:, oc:oc + 1], in1=sil[oc % 2][:, :],
                    op0=ALU.mult, op1=ALU.mult)
                inc(ins, sVE, 1)
        on_v(v12)
        assert val(sVE) == v12_base + 32

        def hstage(wt_f, wt_c, src, dst, sc_sb, src_ready_kind, src_ready):
            th_base = val(sMM)
            mmg = lambda oc: th_base + oc + 1
            sh_base = val(sSC)
            scg = lambda oc: sh_base + oc + 1
            wm = []
            cwh_jobs = []

            def gh(g):
                for ob in range(32):
                    b = ob % 4
                    if ob < 4:
                        g.wait_ge(sMM, th_base)
                    else:
                        g.wait_ge(sMM, mmg(ob - 4))
                    if build_cache and cwcnt[b]:
                        g.wait_ge(sCW[b], cwcnt[b])
                    srcw = (wt_f if build_cache else wt_c)[ob]
                    m = inc(g.dma_start(out=wbuf[b][:, :],
                                        in_=srcw.rearrange("p k c -> p (k c)")),
                            sW[b], 16)
                    wm.append(m)
                    if build_cache:
                        cwcnt[b] += 16
                        cwh_jobs.append((sW[b], m, wbuf[b][:, :],
                                         wt_c[ob].rearrange("p k c -> p (k c)"),
                                         sCW[b]))

            def th(t):
                if src_ready_kind == "ve":
                    t.wait_ge(sVE, src_ready)
                else:
                    t.wait_ge(sSC, src_ready)
                for oc in range(32):
                    t.wait_ge(sW[oc % 4], wm[oc])
                    if oc >= 2:
                        t.wait_ge(sSC, scg(oc - 2))
                    pa = P[2 * (oc % 2)]
                    ins = None
                    for k in range(32):
                        ins = t.matmul(pa[:, :],
                                       wbuf[oc % 4][:, k * 128:(k + 1) * 128],
                                       src[:, k * TC:(k + 1) * TC],
                                       start=(k == 0), stop=(k == 31))
                    inc(ins, sMM, 1)

            def sh(s):
                for oc in range(32):
                    s.wait_ge(sMM, mmg(oc))
                    ins = s.activation(out=dst[:, oc * TC:(oc + 1) * TC],
                                       in_=P[2 * (oc % 2)][:, :], func=AF.Gelu,
                                       scale=sc_sb[:, oc:oc + 1])
                    inc(ins, sSC, 1)

            on_g(gh)
            if build_cache:
                def ycwh(y, jobs=list(cwh_jobs)):
                    for sw, m, srcap, dstap, scw in jobs:
                        y.wait_ge(sw, m)
                        inc(y.dma_start(out=dstap, in_=srcap), scw, 16)
                on_y(ycwh)
            on_t(th)
            on_s(sh)
            assert val(sMM) == th_base + 32
            assert val(sSC) == sh_base + 32
            return scg(31)

        sc_h1 = hstage(h1t, h1c, h_fm, h2_fm, h1s_sb, "ve", ve_h(31))
        sc_h2 = hstage(h2t, h2c, h2_fm, h_fm, h2s_sb, "sc", sc_h1)

        t3_base = val(sMM)
        mm3 = lambda oc: t3_base + (1 if oc == 0 else 2 * oc)
        tr3 = lambda oc: t3_base + (2 * oc + 3 if oc <= 6 else 16)
        s3_base = val(sSC)
        sc3 = lambda oc: s3_base + oc + 1
        v3_base = val(sVE)
        vres = lambda oc: v3_base + oc + 1
        w3m = []
        cw3_jobs = []

        def g3(g):
            for ob in range(8):
                b = ob % 4
                if ob < 4:
                    g.wait_ge(sMM, t3_base)
                else:
                    g.wait_ge(sMM, mm3(ob - 4))
                if build_cache and cwcnt[b]:
                    g.wait_ge(sCW[b], cwcnt[b])
                srcw = (w3t if build_cache else w3c)[ob]
                m = inc(g.dma_start(out=wbuf[b][:, :],
                                    in_=srcw.rearrange("p k c -> p (k c)")),
                        sW[b], 16)
                w3m.append(m)
                if build_cache:
                    cwcnt[b] += 16
                    cw3_jobs.append((sW[b], m, wbuf[b][:, :],
                                     w3c[ob].rearrange("p k c -> p (k c)"),
                                     sCW[b]))
        on_g(g3)
        if build_cache:
            def ycw3(y, jobs=list(cw3_jobs)):
                for sw, m, srcap, dstap, scw in jobs:
                    y.wait_ge(sw, m)
                    inc(y.dma_start(out=dstap, in_=srcap), scw, 16)
            on_y(ycw3)

        def emit_tr(t, oc):
            t.wait_ge(sSC, sc3(oc))
            if oc >= 2:
                t.wait_ge(sVE, vres(oc - 2))
            pb = P[4 + oc % 2]
            ins = None
            for ti in range(NTI):
                ins = t.transpose(pb[:, ti * 128:(ti + 1) * 128],
                                  ffo[oc % 2][:, ti * 128:(ti + 1) * 128],
                                  cI128F[:, :])
            inc(ins, sMM, 1)

        def t3(t):
            t.wait_ge(sSC, sc_h2)
            for oc in range(8):
                t.wait_ge(sW[oc % 4], w3m[oc])
                if oc >= 2:
                    t.wait_ge(sSC, sc3(oc - 2))
                pa = P[2 * (oc % 2)]
                ins = None
                for k in range(32):
                    ins = t.matmul(pa[:, :],
                                   wbuf[oc % 4][:, k * 128:(k + 1) * 128],
                                   h_fm[:, k * TC:(k + 1) * TC],
                                   start=(k == 0), stop=(k == 31))
                inc(ins, sMM, 1)
                if oc >= 1:
                    emit_tr(t, oc - 1)
            emit_tr(t, 7)
        on_t(t3)
        assert val(sMM) == t3_base + 16

        def s3(s):
            s.wait_ge(sSC, s3h_ready)
            for oc in range(8):
                s.wait_ge(sMM, mm3(oc))
                if oc >= 2:
                    s.wait_ge(sMM, tr3(oc - 2))
                ins = s.activation(out=ffo[oc % 2][:, :],
                                   in_=P[2 * (oc % 2)][:, :], func=AF.Copy,
                                   scale=s3h_sb[:, oc:oc + 1])
                inc(ins, sSC, 1)
        on_s(s3)

        def v3(v):
            if res_ready_ve:
                v.wait_ge(sVE, res_ready_ve)
            if res_ready_ld:
                v.wait_ge(res_ready_ld[0], res_ready_ld[1])
            if out_free_ve:
                v.wait_ge(sVE, out_free_ve)
            if out_free_ld:
                v.wait_ge(out_free_ld[0], out_free_ld[1])
            for oc in range(8):
                v.wait_ge(sMM, tr3(oc))
                pb = P[4 + oc % 2]
                ins = None
                for ti in range(NTI):
                    ins = v.tensor_tensor(
                        out=out_tm[:, ti * 1024 + oc * 128: ti * 1024 + oc * 128 + 128],
                        in0=res_tm[:, ti * 1024 + oc * 128: ti * 1024 + oc * 128 + 128],
                        in1=pb[:, ti * 128:(ti + 1) * 128], op=ALU.add)
                inc(ins, sVE, 1)
        on_v(v3)
        assert val(sVE) == v3_base + 8
        out_ve = val(sVE)

        def yst(y):
            y.wait_ge(sVE, out_ve)
            for i in range(NTI):
                ins = y.dma_start(
                    out=out_hbm[ck * TC + i * 128: ck * TC + (i + 1) * 128, :],
                    in_=out_tm[:, i * 1024:(i + 1) * 1024])
                inc(ins, sST, 16)
            if dbg_store is not None:
                for i in range(NTI):
                    inc(y.dma_start(
                        out=dbg_store[ck * TC + i * 128: ck * TC + (i + 1) * 128, :],
                        in_=out_tm[:, i * 1024:(i + 1) * 1024]), sST, 16)
        on_y(yst)
        ffn_state["prev_mm_end"] = t3_base + 16
        ffn_state["t12_end"] = t12_base + 64
        if build_cache:
            ffn_state["cache_wt"] = list(cwcnt)
        return val(sST), out_ve

    esA = ExitStack()
    sbA = lambda n, s, d: esA.enter_context(nc.sbuf_tensor(n, s, d))
    bufs = {
        "lnsrc": [sbA("lnsrc%d" % i, [128, NTI * 1024], F32) for i in range(2)],
        "xout": sbA("xout", [128, NTI * 1024], F32),
        "yf": [sbA("yf%d" % i, [128, 8 * TC], BF16) for i in range(2)],
        "ytile": [sbA("ytileA%d" % i, [128, 1024], BF16) for i in range(4)],
        "sil": [sbA("silA%d" % i, [128, 512], F32) for i in range(2)],
        "ffo": [sbA("ffoA%d" % i, [128, 512], F32) for i in range(2)],
        "h_fm": sbA("h_fmA", [128, 32 * TC], BF16),
        "h2_fm": sbA("h2_fmA", [128, 32 * TC], BF16),
        "wbuf": [sbA("wbufA%d" % i, [128, 4096], BF16) for i in range(4)],
        "sq_ln": sbA("sq_lnA", [128, 1024], F32),
    }

    hist = {}
    for ck in range(NCHUNK):
        par = ck % 2
        lnsrc = bufs["lnsrc"][par]
        yf = bufs["yf"][par]

        def yx(y, ck=ck, lnsrc=lnsrc):
            if ck >= 2:
                y.wait_ge(sVE, hist[(ck - 2, "out_ve")])
            for i in range(NTI):
                inc(y.dma_start(
                    out=lnsrc[:, i * 1024:(i + 1) * 1024],
                    in_=x_own[ck * TC + i * 128: ck * TC + (i + 1) * 128, :]),
                    sX, 16)
        on_y(yx)
        xld = (sX, val(sX))

        ln1_ve = emit_ln(lnsrc, 0, xld, yf,
                         hist.get((ck - 2, "t12_end"), 0),
                         hist.get((ck - 2, "bounce"), 0))
        st_mark, out_ve = emit_ffn(
            yf, ln1_ve, lnsrc, 0, xld, bufs["xout"],
            hist.get((ck - 1, "ln2_ve"), 0),
            (sST, hist[(ck - 1, "store")]) if ck >= 1 else 0,
            xmid_hbm, ck, ck == 0, dbg.get("xmid"))
        hist[(ck, "t12_end")] = ffn_state["t12_end"]
        hist[(ck, "store")] = st_mark
        hist[(ck, "out_ve")] = out_ve
        ln2_ve = emit_ln(bufs["xout"], out_ve, 0, yf,
                         ffn_state["t12_end"], hist.get((ck - 2, "bounce"), 0))
        hist[(ck, "ln2_ve")] = ln2_ve

        def gb(g, ck=ck, ln2_ve=ln2_ve, yf=yf, par=par):
            g.wait_ge(sVE, ln2_ve)
            half_t = cc1_in_t[ck // 2]
            lo = (ck % 2) * TC
            for dc in range(8):
                inc(g.dma_start(
                    out=half_t.ap()[dc, :, lo:lo + TC],
                    in_=yf[:, dc * TC:(dc + 1) * TC]), sCCp[par], 16)
        on_g(gb)
        hist[(ck, "bounce")] = (sCCp[par], val(sCCp[par]))
        if ck == 1:
            b01 = (val(sCCp[0]), val(sCCp[1]))

            def gag1a(g, b01=b01):
                g.wait_ge(sCCp[0], b01[0])
                g.wait_ge(sCCp[1], b01[1])
                inc(g.collective_compute(
                    "AllGather", ALU.bypass,
                    replica_groups=[[0, 1], [2, 3], [4, 5], [6, 7]],
                    ins=[cc1_in_t[0].ap().opt()],
                    outs=[cc1_out_t[0].ap().opt()]), sCL, 1)
            on_g(gag1a)
    bounce1 = (val(sCCp[0]), val(sCCp[1]))
    esA.close()

    esB1 = ExitStack()
    sbB1 = lambda n, s, d: esB1.enter_context(nc.sbuf_tensor(n, s, d))
    k_fm = sbB1("k_fm", [128, 4 * T], BF16)
    q_fm = sbB1("q_fm", [128, 4 * T], BF16)
    v_tm = sbB1("v_tm", [128, 32 * E], BF16)
    cum_hd = sbB1("cum_hd", [8, T], F32)
    beta_hd = sbB1("beta_hd", [8, T], BF16)

    esB2 = ExitStack()
    sbB2 = lambda n, s, d: esB2.enter_context(nc.sbuf_tensor(n, s, d))
    y2buf = sbB2("y2buf", [128, 8 * 1024], BF16)
    cum_hd2 = sbB2("cum_hd2", [8, T], F32)
    wk_sb = sbB2("wk_sb", [128, 8 * E], BF16)
    wv_sb = sbB2("wv_sb", [128, 8 * E], BF16)
    wq_sb = sbB2("wq_sb", [128, 8 * E], BF16)
    wbeta_sb = sbB2("wbeta_sb", [128, 8 * H], BF16)
    kq_scr = [sbB2("kq_scr%d" % p, [128, 512], F32) for p in range(2)]
    sqp_scr = [sbB2("sqp_scr%d" % p, [128, 512], F32) for p in range(2)]
    nrm_scr0 = sbB2("nrm_scr", [2, 512], F32)
    nrm_scr = [nrm_scr0, nrm_scr0]
    nrm2_scr = [sbB2("nrm2_scr%d" % p, [2, 512], F32) for p in range(2)]

    ph1_stores = val(sST)

    def ph2(g):
        g.wait_ge(sCCp[0], bounce1[0])
        g.wait_ge(sCCp[1], bounce1[1])
        g.wait_ge(sST, ph1_stores)
        i = g.collective_compute(
            "AllGather", ALU.bypass,
            replica_groups=[[0, 1], [2, 3], [4, 5], [6, 7]],
            ins=[cc1_in_t[1].ap().opt()], outs=[cc1_out_t[1].ap().opt()])
        inc(i, sCL, 1)
        g.wait_ge(sCL, val(sCL))
        for dst, src in [(wk_sb, wk_d), (wv_sb, wv_d), (wq_sb, wq_d),
                         (wbeta_sb, wbeta_d)]:
            inc(g.dma_start(out=dst[:, :],
                            in_=src.rearrange("p c e -> p (c e)")), sPW, 16)
    on_g(ph2)
    ag1_done = val(sCL)
    pw_ld = val(sPW)

    if "y2" in dbg:
        def dy2(g):
            g.wait_ge(sCL, ag1_done)
            for dc in range(8):
                for half in range(2):
                    for hb in range(2):
                        inc(g.dma_start(
                            out=dbg["y2"][dc, :, half * TOK + hb * 1024:
                                          half * TOK + (hb + 1) * 1024],
                            in_=cc1_out_t[hb].ap()[half, dc]), sAT, 16)
        on_g(dy2)

    def load_y2_quarter(g, qi):
        src_t = cc1_out_t[qi % 2]
        g.wait_ge(sCL, ag1_done - 1 + (qi % 2))
        half = qi // 2
        for dc in range(8):
            inc(g.dma_start(out=y2buf[:, dc * 1024:(dc + 1) * 1024],
                            in_=src_t.ap()[half, dc]),
                sY2, 16)
        return val(sY2)

    kqstate = {"buffree": 0, "idx": 0, "marks": {}}

    def emit_kq(dst_fm, w_src, is_q):
        M = kqstate["marks"]
        for qi in range(4):
            qm = [0]

            def gq(g, qi=qi, qm=qm):
                if kqstate["buffree"]:
                    g.wait_ge(sMM, kqstate["buffree"])
                qm[0] = load_y2_quarter(g, qi)
            on_g(gq)
            for ec in range(4):
                for tsl in range(2):
                    i = kqstate["idx"]
                    kqstate["idx"] += 1
                    p = i % 2
                    Pp, Pn = P[0 + 2 * p], P[1 + 2 * p]

                    def tp(t, ec=ec, tsl=tsl, qm=qm, i=i, p=p, Pp=Pp,
                           last=(ec == 3 and tsl == 1)):
                        t.wait_ge(sY2, qm[0])
                        t.wait_ge(sPW, pw_ld)
                        if (i - 2, "vf") in M:
                            t.wait_ge(sVE, M[(i - 2, "vf")])
                        ins = None
                        for d in range(8):
                            ins = t.matmul(
                                Pp[:, :],
                                w_src[:, d * E + ec * 128: d * E + ec * 128 + 128],
                                y2buf[:, d * 1024 + tsl * 512: d * 1024 + (tsl + 1) * 512],
                                start=(d == 0), stop=(d == 7))
                        M[(i, "mm")] = inc(ins, sMM, 1)
                        if last:
                            kqstate["buffree"] = val(sMM)
                    on_t(tp)

                    def sp(s, i=i, p=p, Pp=Pp):
                        s.wait_ge(sMM, M[(i, "mm")])
                        if (i - 2, "ind") in M:
                            s.wait_ge(sMM, M[(i - 2, "ind")])
                        M[(i, "sq")] = inc(
                            s.activation(out=sqp_scr[p][:, :], in_=Pp[:, :],
                                         func=AF.Square), sSC, 1)
                    on_s(sp)

                    def ti(t, i=i, p=p, Pn=Pn):
                        t.wait_ge(sSC, M[(i, "sq")])
                        if (i - 2, "bc") in M:
                            t.wait_ge(sSC, M[(i - 2, "ev")])
                        M[(i, "ind")] = inc(
                            t.matmul(Pn[0:2, :], cIND64[:, :],
                                     sqp_scr[p][:, :], start=True, stop=True),
                            sMM, 1)
                    on_t(ti)

                    def sn(s, i=i, p=p, Pn=Pn, is_q=is_q):
                        s.wait_ge(sMM, M[(i, "ind")])
                        s.activation(out=nrm_scr[0][0:2, :], in_=Pn[0:2, :],
                                     func=AF.Ln, bias=cv[0:2, CV_T24:CV_T24 + 1])
                        s.drain()
                        M[(i, "rc")] = inc(
                            s.activation(out=nrm2_scr[p][0:2, :],
                                         in_=nrm_scr[0][0:2, :], func=AF.Exp,
                                         scale=cv[0:2, CV_NHALF:CV_NHALF + 1],
                                         bias=cv[0:2, CV_LN8:CV_LN8 + 1]
                                         if is_q else
                                         cv[0:2, CV_Z2:CV_Z2 + 1]), sSC, 1)
                    on_s(sn)

                    def tb(t, i=i, p=p, Pn=Pn):
                        t.wait_ge(sSC, M[(i, "rc")])
                        M[(i, "bc")] = inc(
                            t.matmul(Pn[:, :], cINDT[:, :],
                                     nrm2_scr[p][0:2, :], start=True,
                                     stop=True), sMM, 1)
                    on_t(tb)

                    def se(s, i=i, p=p, Pn=Pn):
                        s.wait_ge(sMM, M[(i, "bc")])
                        if (i - 2, "vf") in M:
                            s.wait_ge(sVE, M[(i - 2, "vf")])
                        M[(i, "ev")] = inc(
                            s.activation(out=kq_scr[p][:, :], in_=Pn[:, :],
                                         func=AF.Copy), sSC, 1)
                    on_s(se)

                    def vf(v, i=i, p=p, Pp=Pp, ec=ec, qi=qi, tsl=tsl,
                           dst_fm=dst_fm):
                        v.wait_ge(sSC, M[(i, "ev")])
                        col = ec * T + qi * 1024 + tsl * 512
                        M[(i, "vf")] = inc(
                            v.tensor_tensor(out=dst_fm[:, col:col + 512],
                                            in0=kq_scr[p][:, :],
                                            in1=Pp[:, :], op=ALU.mult), sVE, 1)
                    on_v(vf)

    emit_kq(k_fm, wk_sb, False)
    emit_kq(q_fm, wq_sb, True)

    for qi in range(4):
        qm = [0]
        tv_base = val(sMM)
        vq_base = val(sVE)

        def gv(g, qi=qi, qm=qm):
            if kqstate["buffree"]:
                g.wait_ge(sMM, kqstate["buffree"])
            qm[0] = load_y2_quarter(g, qi)
        on_g(gv)

        tv_base = val(sMM)
        vq_base = val(sVE)
        vmm = lambda lt: tv_base + lt + 1
        vfr = lambda lt: vq_base + lt + 1

        def tvp(t, qi=qi, qm=qm, vmm=vmm, vfr=vfr):
            t.wait_ge(sY2, qm[0])
            t.wait_ge(sPW, pw_ld)
            if kqstate["marks"]:
                t.wait_ge(sVE, kqstate["marks"][(kqstate["idx"] - 1, "vf")])
            for lt in range(8):
                tt = qi * 8 + lt
                if lt >= 2:
                    t.wait_ge(sVE, vfr(lt - 2))
                elif qi >= 1 and kqstate.get("v_prev_ve"):
                    t.wait_ge(sVE, kqstate["v_prev_ve"] - 2 + lt + 1)
                pv = P[2 + tt % 2]
                ins = None
                for d in range(8):
                    lhs = y2buf[:, d * 1024 + lt * 128: d * 1024 + (lt + 1) * 128]
                    t.matmul(pv[:, :], lhs, wv_sb[:, d * E:(d + 1) * E],
                             start=(d == 0), stop=(d == 7))
                    ins = t.matmul(P[4][:, tt * 8:(tt + 1) * 8], lhs,
                                   wbeta_sb[:, d * H:(d + 1) * H],
                                   start=(d == 0), stop=(d == 7))
                inc(ins, sMM, 1)
            kqstate["buffree"] = val(sMM)
        on_t(tvp)
        assert val(sMM) == tv_base + 8

        def vvp(v, qi=qi, vmm=vmm):
            for lt in range(8):
                tt = qi * 8 + lt
                v.wait_ge(sMM, vmm(lt))
                inc(v.tensor_copy(out=v_tm[:, tt * E:(tt + 1) * E],
                                  in_=P[2 + tt % 2][:, :]), sVE, 1)
        on_v(vvp)
        assert val(sVE) == vq_base + 8
        kqstate["v_prev_ve"] = val(sVE)
    v_done = val(sVE)
    beta_mm = val(sMM)

    def sbeta(s):
        s.wait_ge(sMM, beta_mm)
        s.activation(out=beta_tm[:, :], in_=P[4][:, 0:256], func=AF.Sigmoid)
        s.drain()
        inc(s.activation(out=lr_tm[:, :], in_=beta_tm[:, :], func=AF.Ln,
                         scale=cv[:, CV_NEG1:CV_NEG1 + 1],
                         bias=cv[:, CV_1P:CV_1P + 1]), sSC, 1)
    on_s(sbeta)
    lr_done = val(sSC)

    def tcum(t):
        t.wait_ge(sSC, lr_done)
        for tt in range(NTT):
            if tt >= 1:
                t.wait_ge(sVE, cum_ve0 + tt)
            t.transpose(P[4][0:8, 0:128], lr_tm[:, tt * 8:(tt + 1) * 8],
                        cI128F[:, :])
            inc(t.transpose(P[5][0:8, 0:128], beta_tm[:, tt * 8:(tt + 1) * 8],
                            cI128F[:, :]), sMM, 1)
    tr_b = val(sMM)
    cum_ve0 = val(sVE)
    on_t(tcum)
    assert val(sMM) == tr_b + NTT

    def vcum(v):
        for tt in range(NTT):
            v.wait_ge(sMM, tr_b + tt + 1)
            v.tensor_copy(out=cum_hd[:, tt * 128:(tt + 1) * 128],
                          in_=P[4][0:8, 0:128])
            inc(v.tensor_copy(out=beta_hd[:, tt * 128:(tt + 1) * 128],
                              in_=P[5][0:8, 0:128]), sVE, 1)
        bufs_ = [cum_hd, cum_hd2]
        k = 1
        while k < T:
            a, b = bufs_[0], bufs_[1]
            v.drain()
            v.tensor_copy(out=b[:, 0:k], in_=a[:, 0:k])
            v.tensor_tensor(out=b[:, k:T], in0=a[:, k:T], in1=a[:, 0:T - k],
                            op=ALU.add)
            bufs_ = [b, a]
            k *= 2
        assert bufs_[0] is cum_hd
        v.drain()
        inc(v.tensor_copy(out=cum_hd[:, 0:1], in_=cum_hd[:, 0:1]), sVE, 1)
    on_v(vcum)
    cum_hdve = val(sVE)

    tr_b2 = val(sMM)

    def tcum2(t):
        t.wait_ge(sVE, cum_hdve)
        for tt in range(NTT):
            if tt >= 2:
                t.wait_ge(sVE, cum_hdve + tt - 1)
            inc(t.transpose(P[4 + tt % 2][0:128, 0:8],
                            cum_hd[:, tt * 128:(tt + 1) * 128],
                            cI128F[0:8, 0:8]), sMM, 1)
    on_t(tcum2)
    assert val(sMM) == tr_b2 + NTT

    def vcum2(v):
        for tt in range(NTT):
            v.wait_ge(sMM, tr_b2 + tt + 1)
            inc(v.tensor_copy(out=cum_tm[:, tt * 8:(tt + 1) * 8],
                              in_=P[4 + tt % 2][0:128, 0:8]), sVE, 1)
    on_v(vcum2)
    prep_ve = val(sVE)

    for nm, src in [("cum", cum_tm), ("beta", beta_tm)]:
        if nm in dbg:
            def dct(g, nm=nm, src=src):
                g.wait_ge(sVE, prep_ve)
                inc(g.dma_start(out=dbg[nm][:, :], in_=src[:, :]), sAT, 16)
            on_g(dct)
    if any(n in dbg for n in ("k", "q", "v")):
        def dkqv(g):
            g.wait_ge(sVE, prep_ve)
            if "k" in dbg:
                for ec in range(4):
                    inc(g.dma_start(out=dbg["k"][ec],
                                    in_=k_fm[:, ec * T:(ec + 1) * T]), sAT, 16)
            if "q" in dbg:
                for ec in range(4):
                    inc(g.dma_start(out=dbg["q"][ec],
                                    in_=q_fm[:, ec * T:(ec + 1) * T]), sAT, 16)
            if "v" in dbg:
                for tt in range(32):
                    inc(g.dma_start(out=dbg["v"][tt],
                                    in_=v_tm[:, tt * E:(tt + 1) * E]), sAT, 16)
        on_g(dkqv)

    esB2.close()

    esC = ExitStack()
    sbC = lambda n, s, d: esC.enter_context(nc.sbuf_tensor(n, s, d))
    attn_fm = sbC("attn_fm", [128, 4 * T], BF16)
    ct_sb = [sbC("ct%d" % i, [128, 512], F32) for i in range(2)]
    d_sb = [[sbC("d%d_%d" % (i, p), [128, 512], F32) for p in range(2)]
            for i in range(2)]
    bb_sb = [sbC("bb%d" % i, [128, 512], BF16) for i in range(2)]
    e_sb = [[sbC("e%d_%d" % (i, p), [128, 512], F32) for p in range(2)]
            for i in range(2)]
    w_sb = [[sbC("w%d_%d" % (i, p), [128, 512], BF16) for p in range(2)]
            for i in range(2)]

    prep_free = [prep_ve]
    for tr in range(8):
        t0 = tr * 512
        s_lo = tr * 4
        n_st = NTT - s_lo
        for hp in range(4):
            h0, h1 = 2 * hp, 2 * hp + 1
            ctb = val(sMM)
            sc_p = val(sSC)

            def tprep(t, t0=t0, h0=h0, h1=h1, ctb=ctb, sc_p=sc_p,
                      pf=prep_free[0]):
                t.wait_ge(sVE, pf)
                kk = 0
                for sel, src_hd in ((cSEL8, cum_hd), (cSEL8B, beta_hd)):
                    for h in (h0, h1):
                        if kk >= 2:
                            t.wait_ge(sSC, sc_p + kk - 1)
                        inc(t.matmul(P[4 + kk % 2][:, :],
                                     sel[:, h * 128:(h + 1) * 128],
                                     src_hd[0:8, t0: t0 + 512],
                                     start=True, stop=True), sMM, 1)
                        kk += 1
            on_t(tprep)
            assert val(sMM) == ctb + 4

            def sprep(s, ctb=ctb):
                kk = 0
                for dsts in (ct_sb, bb_sb):
                    for hh in range(2):
                        s.wait_ge(sMM, ctb + kk + 1)
                        inc(s.activation(out=dsts[hh][:, :],
                                         in_=P[4 + kk % 2][:, :],
                                         func=AF.Copy), sSC, 1)
                        kk += 1
            on_s(sprep)
            assert val(sSC) == sc_p + 4

            def vprep(v, sc4=val(sSC)):
                v.wait_ge(sSC, sc4)
                for hh in range(2):
                    inc(v.tensor_tensor(out=bb_sb[hh][:, :],
                                        in0=bb_sb[hh][:, :],
                                        in1=cIREP8[:, 0:512], op=ALU.mult),
                        sVE, 1)
            on_v(vprep)

            mm_base = val(sMM)
            sc_base = val(sSC)
            ve_base = val(sVE)

            def t_pre(si):
                return 0 if si == 0 else 2 + 4 * (si - 1)

            def kq_m(si, hh):
                return mm_base + t_pre(si) + (2 if si >= 1 else 0) + hh + 1

            def av_m(si, hh):
                if si < n_st - 1:
                    return mm_base + t_pre(si + 1) + hh + 1
                return mm_base + t_pre(n_st - 1) + 4 + hh + 1

            def e_m(si, hh):
                return sc_base + 2 * si + hh + 1

            def v_pre(si):
                return 0 if si == 0 else 4 * si - 2

            def d_m(si, hh):
                return ve_base + v_pre(si) + hh + 1

            def w_m(si, hh):
                if si < n_st - 1:
                    return ve_base + v_pre(si + 1) + 2 + hh + 1
                return ve_base + 4 * n_st - 2 + hh + 1

            def emit_av(t, si, hp, n_st):
                stp = ((tr * 4) + si)
                for hh in range(2):
                    t.wait_ge(sVE, w_m(si, hh))
                    po = P[2 + hh][64 * hh:64 * hh + 64, :]
                    inc(t.matmul(po,
                                 v_tm[:, stp * E + (2 * hp + hh) * DM:
                                      stp * E + (2 * hp + hh) * DM + 64],
                                 w_sb[hh][si % 2][:, :],
                                 start=(si == 0), stop=(si == n_st - 1),
                                 tile_position=(0, 64 * hh)), sMM, 1)

            def tmain(t, t0=t0, s_lo=s_lo, h0=h0, h1=h1, hp=hp, n_st=n_st,
                      emit_av=emit_av, bb_ve=ve_base):
                t.wait_ge(sVE, v_done)
                t.wait_ge(sVE, bb_ve)
                for si in range(n_st):
                    st = s_lo + si
                    if si >= 1:
                        emit_av(t, si - 1, hp, n_st)
                    band = si < 4
                    doff = si * 128
                    for hh, h in enumerate((h0, h1)):
                        ec = h // 2
                        off = 64 * (h % 2)
                        ins = t.matmul(
                            P[hh][:, :],
                            k_fm[off:off + 64, ec * T + st * 128: ec * T + (st + 1) * 128],
                            q_fm[off:off + 64, ec * T + t0: ec * T + t0 + 512],
                            start=True, stop=not band)
                        if band:
                            ins = t.matmul(
                                P[hh][:, doff:doff + 128], cI128B[:, :],
                                bb_sb[hh][:, doff:doff + 128],
                                start=False, stop=True)
                        inc(ins, sMM, 1)
                emit_av(t, n_st - 1, hp, n_st)
            on_t(tmain)
            assert val(sMM) == mm_base + 4 * n_st

            def smain(s, n_st=n_st, w_m=w_m, d_m=d_m):
                for si in range(n_st):
                    for hh in range(2):
                        s.wait_ge(sVE, d_m(si, hh))
                        if si >= 2:
                            s.wait_ge(sVE, w_m(si - 2, hh))
                        else:
                            s.drain()
                        inc(s.activation(out=e_sb[hh][si % 2][:, :],
                                         in_=d_sb[hh][si % 2][:, :],
                                         func=AF.Exp),
                            sSC, 1)
            on_s(smain)
            assert val(sSC) == sc_base + 2 * n_st

            def emit_w(v, si, n_st):
                off = si * 128
                for hh in range(2):
                    v.wait_ge(sSC, e_m(si, hh))
                    v.wait_ge(sMM, kq_m(si, hh))
                    wd = w_sb[hh][si % 2]
                    if si < 4:
                        ins = v.scalar_tensor_tensor(
                            out=wd[:, 0:off + 128],
                            in0=e_sb[hh][si % 2][:, 0:off + 128],
                            scalar=1.0, in1=P[hh][:, 0:off + 128],
                            op0=ALU.min, op1=ALU.mult)
                        v.drain()
                        ins = v.tensor_tensor(out=wd[:, off:off + 128],
                                              in0=wd[:, off:off + 128],
                                              in1=cTRILB[:, :], op=ALU.mult)
                    else:
                        ins = v.scalar_tensor_tensor(
                            out=wd[:, :], in0=e_sb[hh][si % 2][:, :],
                            scalar=1.0, in1=P[hh][:, :],
                            op0=ALU.min, op1=ALU.mult)
                    inc(ins, sVE, 1)

            def vmain(v, s_lo=s_lo, h0=h0, h1=h1, n_st=n_st, kq_m=kq_m,
                      e_m=e_m, emit_w=emit_w, pf=prep_free[0]):
                for hh in range(2):
                    for p in range(2):
                        v.memset(w_sb[hh][p][:, (p + 1) * 128:512], 0.0)
                for si in range(n_st):
                    st = s_lo + si
                    for hh, h in enumerate((h0, h1)):
                        if si >= 2:
                            v.wait_ge(sSC, e_m(si - 2, hh))
                        elif si == 0 and hh == 0:
                            v.drain()
                        inc(v.tensor_scalar(
                            out=d_sb[hh][si % 2][:, :], in0=ct_sb[hh][:, :],
                            scalar1=cum_tm[:, st * 8 + h: st * 8 + h + 1],
                            scalar2=0.0, op0=ALU.subtract, op1=ALU.min),
                            sVE, 1)
                    if si >= 1:
                        emit_w(v, si - 1, n_st)
                emit_w(v, n_st - 1, n_st)
            on_v(vmain)
            assert val(sVE) == ve_base + 4 * n_st

            av_last = val(sMM)

            def vav(v, hp=hp, t0=t0, av_last=av_last):
                v.wait_ge(sMM, av_last)
                v.tensor_copy(out=attn_fm[0:64, hp * T + t0: hp * T + t0 + 512],
                              in_=P[2][0:64, :])
                prep_free[0] = inc(v.tensor_copy(
                    out=attn_fm[64:128, hp * T + t0: hp * T + t0 + 512],
                    in_=P[3][64:128, :]), sVE, 1)
            on_v(vav)
    attn_ve = val(sVE)

    if "attnout" in dbg:
        def datt(g):
            g.wait_ge(sVE, attn_ve)
            for ec in range(4):
                inc(g.dma_start(out=dbg["attnout"][ec],
                                in_=attn_fm[:, ec * T:(ec + 1) * T]), sAT, 16)
        on_g(datt)

    def ph5(g):
        g.wait_ge(sVE, attn_ve)
        for ec in range(4):
            inc(g.dma_start(out=cc2_in_t.ap()[ec],
                            in_=attn_fm[:, ec * T:(ec + 1) * T]), sCC2, 16)
        g.wait_ge(sCC2, val(sCC2))
        i = g.collective_compute(
            "AllGather", ALU.bypass,
            replica_groups=[[0, 1], [2, 3], [4, 5], [6, 7]],
            ins=[cc2_in_t.ap().opt()], outs=[cc2_out_t.ap().opt()])
        inc(i, sCL, 1)
    on_g(ph5)
    a2a_done = val(sCL)
    esC.close()
    esB1.close()

    esD = ExitStack()
    sbD = lambda n, s, d: esD.enter_context(nc.sbuf_tensor(n, s, d))
    attnA = sbD("attnA", [128, 4 * TOK], BF16)
    attnB = sbD("attnB", [128, 4 * TOK], BF16)
    wout_sb = sbD("wout_sb", [128, 8 * D], BF16)
    xm_t = [sbD("xm%d" % i, [128, 1024], F32) for i in range(2)]
    x2_t = [sbD("x2%d" % i, [128, 1024], F32) for i in range(2)]

    def ph6g(g):
        g.wait_ge(sCL, a2a_done)
        pid = g.partition_id()
        off = (pid % 2) * TOK
        for ec in range(4):
            inc(g.dma_start(out=attnA[:, ec * TOK:(ec + 1) * TOK],
                            in_=cc2_out_t.ap()[0, ec][:, ds(off, TOK)]),
                sAT, 16)
            inc(g.dma_start(out=attnB[:, ec * TOK:(ec + 1) * TOK],
                            in_=cc2_out_t.ap()[1, ec][:, ds(off, TOK)]),
                sAT, 16)
        inc(g.dma_start(out=wout_sb[:, :],
                        in_=wout_d.rearrange("p c e -> p (c e)")), sAT, 16)
    on_g(ph6g)
    a2a_ld = val(sAT)

    if "attnfull" in dbg:
        def datf(g):
            g.wait_ge(sAT, a2a_ld)
            for ec in range(4):
                inc(g.dma_start(out=dbg["attnfull"][ec],
                                in_=attnA[:, ec * TOK:(ec + 1) * TOK]), sAT, 16)
                inc(g.dma_start(out=dbg["attnfull"][4 + ec],
                                in_=attnB[:, ec * TOK:(ec + 1) * TOK]), sAT, 16)
        on_g(datf)

    xm_marks, wo_marks, x2v_marks, x2st_marks = {}, {}, {}, {}
    ph6_st_gate = val(sST)
    for tt in range(16):
        def ywo(y, tt=tt):
            if tt == 0:
                y.wait_ge(sST, ph6_st_gate)
                y.wait_ge(sAT, a2a_ld)
            if tt >= 2:
                y.wait_ge(sVE, x2v_marks[tt - 2])
            xm_marks[tt] = inc(y.dma_start(
                out=xm_t[tt % 2][:, :],
                in_=xmid_hbm[tt * 128:(tt + 1) * 128, :]), sXM[tt % 2], 16)
        on_y(ywo)

        def two(t, tt=tt):
            t.wait_ge(sAT, a2a_ld)
            if tt >= 2:
                t.wait_ge(sVE, x2v_marks[tt - 2])
            pa, pb = P[2 * (tt % 2)], P[2 * (tt % 2) + 1]
            for ec in range(4):
                lhsA = attnA[:, ec * TOK + tt * 128: ec * TOK + (tt + 1) * 128]
                t.matmul(pa[:, :], lhsA, wout_sb[:, ec * D: ec * D + 512],
                         start=(ec == 0), stop=False)
                t.matmul(pb[:, :], lhsA, wout_sb[:, ec * D + 512:(ec + 1) * D],
                         start=(ec == 0), stop=False)
            i2 = None
            for ec in range(4):
                lhsB = attnB[:, ec * TOK + tt * 128: ec * TOK + (tt + 1) * 128]
                t.matmul(pa[:, :], lhsB,
                         wout_sb[:, (4 + ec) * D: (4 + ec) * D + 512],
                         start=False, stop=(ec == 3))
                i2 = t.matmul(pb[:, :], lhsB,
                              wout_sb[:, (4 + ec) * D + 512:(5 + ec) * D],
                              start=False, stop=(ec == 3))
            wo_marks[tt] = inc(i2, sMM, 1)
        on_t(two)

        def vwo(v, tt=tt):
            v.wait_ge(sMM, wo_marks[tt])
            v.wait_ge(sXM[tt % 2], xm_marks[tt])
            if tt >= 2:
                v.wait_ge(sX2[tt % 2], x2st_marks[tt - 2])
            pa, pb = P[2 * (tt % 2)], P[2 * (tt % 2) + 1]
            v.tensor_tensor(out=x2_t[tt % 2][:, 0:512],
                            in0=xm_t[tt % 2][:, 0:512], in1=pa[:, :],
                            op=ALU.add)
            x2v_marks[tt] = inc(v.tensor_tensor(
                out=x2_t[tt % 2][:, 512:1024],
                in0=xm_t[tt % 2][:, 512:1024], in1=pb[:, :], op=ALU.add),
                sVE, 1)
        on_v(vwo)

        def yx2(y, tt=tt):
            y.wait_ge(sVE, x2v_marks[tt])
            x2st_marks[tt] = inc(y.dma_start(
                out=x2_hbm[tt * 128:(tt + 1) * 128, :],
                in_=x2_t[tt % 2][:, :]), sX2[tt % 2], 16)
            if "x2" in dbg:
                inc(y.dma_start(out=dbg["x2"][tt * 128:(tt + 1) * 128, :],
                                in_=x2_t[tt % 2][:, :]), sST, 16)
        on_y(yx2)
    x2_final = (val(sX2[0]), val(sX2[1]))
    esD.close()

    esE = ExitStack()
    sbE = lambda n, s, d: esE.enter_context(nc.sbuf_tensor(n, s, d))
    bufs = {
        "lnsrc": [sbE("lnsrc3%d" % i, [128, NTI * 1024], F32) for i in range(2)],
        "xout": sbE("xout3", [128, NTI * 1024], F32),
        "yf": [sbE("yf3%d" % i, [128, 8 * TC], BF16) for i in range(2)],
        "ytile": [sbE("ytileE%d" % i, [128, 1024], BF16) for i in range(4)],
        "sil": [sbE("silE%d" % i, [128, 512], F32) for i in range(2)],
        "ffo": [sbE("ffoE%d" % i, [128, 512], F32) for i in range(2)],
        "h_fm": sbE("h_fmE", [128, 32 * TC], BF16),
        "h2_fm": sbE("h2_fmE", [128, 32 * TC], BF16),
        "wbuf": [sbE("wbufE%d" % i, [128, 4096], BF16) for i in range(4)],
        "sq_ln": sbE("sq_lnE", [128, 1024], F32),
    }

    hist3 = {}
    for ck in range(NCHUNK):
        par = ck % 2
        lnsrc = bufs["lnsrc"][par]
        yf = bufs["yf"][par]

        def yx3(y, ck=ck, lnsrc=lnsrc):
            if ck == 0:
                y.wait_ge(sX2[0], x2_final[0])
                y.wait_ge(sX2[1], x2_final[1])
            if ck >= 2:
                y.wait_ge(sVE, hist3[(ck - 2, "out_ve")])
            for i in range(NTI):
                inc(y.dma_start(
                    out=lnsrc[:, i * 1024:(i + 1) * 1024],
                    in_=x2_hbm[ck * TC + i * 128: ck * TC + (i + 1) * 128, :]),
                    sX, 16)
        on_y(yx3)
        xld = (sX, val(sX))

        ln3_ve = emit_ln(lnsrc, 0, xld, yf,
                         hist3.get((ck - 2, "t12_end"), 0), 0)
        st_mark, out_ve = emit_ffn(
            yf, ln3_ve, lnsrc, 0, xld, bufs["xout"],
            0, (sST, hist3[(ck - 1, "store")]) if ck >= 1 else 0,
            y_out, ck, False, None)
        hist3[(ck, "t12_end")] = ffn_state["t12_end"]
        hist3[(ck, "out_ve")] = out_ve
        hist3[(ck, "store")] = st_mark

    finals_y = [(sST, val(sST)), (sX, val(sX)), (sC, val(sC)),
                (sX2[0], val(sX2[0])), (sX2[1], val(sX2[1])),
                (sXM[0], val(sXM[0])), (sXM[1], val(sXM[1]))]
    finals_g = [(sAT, val(sAT)), (sY2, val(sY2)),
                (sCCp[0], val(sCCp[0])), (sCCp[1], val(sCCp[1])),
                (sCC2, val(sCC2)),
                (sPW, val(sPW))] + [(sW[i], val(sW[i])) for i in range(4)] + \
               [(sCW[i], val(sCW[i])) for i in range(4)] +                [(sXM[i], val(sXM[i])) for i in range(2)]

    def yfin(y):
        for s, v in finals_y:
            if v:
                y.wait_ge(s, v)
    on_y(yfin)

    def gfin(g):
        for s, v in finals_g:
            if v:
                g.wait_ge(s, v)
    on_g(gfin)

    esE.close()
    es0.close()
    return nc


_CACHED = {}


def kernel(**inputs):
    in_maps = host_prepare(inputs)
    if "nc" not in _CACHED:
        _CACHED["nc"] = build_nc()
    nc = _CACHED["nc"]
    res = run_bass_kernel_spmd(nc, in_maps, core_ids=list(range(8)))
    out = np.empty((B_, T, D), dtype=np.float32)
    for c in range(8):
        b, j = c // 2, c % 2
        out[b, j * TOK:(j + 1) * TOK, :] = res.results[c]["y_out"]
    return out
